# version 04
# baseline (speedup 1.0000x reference)
import math
import numpy as np
import ml_dtypes
import concourse.bass as bass
import concourse.mybir as mybir
from concourse.bass_utils import run_bass_kernel_spmd
from concourse.bass import IndirectOffsetOnAxis

F32 = mybir.dt.float32
BF16 = mybir.dt.bfloat16
I32 = mybir.dt.int32
AF = mybir.ActivationFunctionType
ALU = mybir.AluOpType
AX = mybir.AxisListType
NBF = ml_dtypes.bfloat16

NCORES = 8
N_TOK = 16384
TPC = N_TOK // NCORES
D = 1024


class Tok:
    __slots__ = ("sem", "val")
    def __init__(self, sem, val):
        self.sem, self.val = sem, val

DMA_OPS = ("dma_start", "indirect_dma_start", "collective_compute", "dma_start_transpose")

class EngRec:
    def __init__(self, prog, key):
        self.prog, self.key = prog, key
        self.items = []
        self.cnt = 0
    def __getattr__(self, op):
        if op.startswith("_"):
            raise AttributeError(op)
        def f(*args, dep=None, **kwargs):
            tok = Tok(None, None)
            self.items.append((op, args, kwargs, dep, tok))
            return tok
        return f

class Prog:
    def __init__(self, name):
        self.nc = bass.Bass("TRN2", target_bir_lowering=False, name=name)
        self.name = name
        self._ctx = []
        self._consts = {}

    def constv(self, value, dtype=F32):
        key = (dtype, float(value))
        if key not in self._consts:
            ap = self.sb(f"const{len(self._consts)}", [128, 1], dtype)
            self.nc.const_aps.register(dtype, float(value), ap[:, 0:1])
            self._consts[key] = ap
        return self._consts[key]

    def din(self, name, shape, dtype=F32):
        return self.nc.dram_tensor(name, list(shape), dtype, kind="ExternalInput").ap()
    def dout(self, name, shape, dtype=F32):
        return self.nc.dram_tensor(name, list(shape), dtype, kind="ExternalOutput").ap()
    def sb(self, name, shape, dtype=F32):
        h = self.nc.sbuf_tensor("sb_" + name, list(shape), dtype)
        self._ctx.append(h)
        return h.__enter__()
    def ps(self, name, shape, dtype=F32):
        h = self.nc.psum_tensor("ps_" + name, list(shape), dtype)
        self._ctx.append(h)
        return h.__enter__()

    def build(self, fn):
        class Px:
            pass
        px = Px()
        for k in ("t", "v", "s", "g", "sy"):
            setattr(px, k, EngRec(self, k))
        fn(px)
        ncst = len(self._consts)
        cgate = None
        if ncst:
            citems = []
            for (dt_, val), ap in self._consts.items():
                tok = Tok(None, None)
                citems.append(("memset", (ap[:, :], val), {}, None, tok))
            cgate = citems[-1][4]
            px.g.items = citems + px.g.items
        KDMA = 8
        cnts = {}
        dmai = {}
        self._dma_prewait = {}
        for k in ("t", "v", "s", "g", "sy"):
            for (op, args, kwargs, dep, tok) in getattr(px, k).items:
                if op in DMA_OPS:
                    i = dmai.get(k, 0)
                    dmai[k] = i + 1
                    sk = f"{k}D{i % KDMA}"
                    lap = i // KDMA
                    cnts[sk] = cnts.get(sk, 0) + 16
                    tok.sem, tok.val = sk, 16 * (lap + 1)
                    if lap > 0:
                        self._dma_prewait[id(tok)] = (sk, 16 * lap)
                else:
                    sk = k + "C"
                    cnts[sk] = cnts.get(sk, 0) + 1
                    tok.sem, tok.val = sk, cnts[sk]
        if cgate is not None:
            for k in ("t", "v", "s", "sy"):
                items = getattr(px, k).items
                if items:
                    op, args, kwargs, dep, tok = items[0]
                    deps = dep if isinstance(dep, (list, tuple)) else ([dep] if dep is not None else [])
                    items[0] = (op, args, kwargs, list(deps) + [cgate], tok)

        block_h = self.nc.Block()
        block = block_h.__enter__()
        self._ctx.append(block_h)
        sems = {}
        for sk in cnts:
            h = self.nc.semaphore("sem_" + sk)
            self._ctx.append(h)
            sems[sk] = h.__enter__()
        engmap = {"t": "tensor", "v": "vector", "s": "scalar", "g": "gpsimd", "sy": "sync"}

        prewait = self._dma_prewait

        def mk_emit(rec, key):
            def emit(eng):
                seen = {}
                for (op, args, kwargs, dep, tok) in rec.items:
                    is_dma = op in DMA_OPS
                    deps = dep if isinstance(dep, (list, tuple)) else ([dep] if dep is not None else [])
                    pw = prewait.get(id(tok))
                    if pw is not None and seen.get(pw[0], -1) < pw[1]:
                        eng.wait_ge(sems[pw[0]], pw[1])
                        seen[pw[0]] = pw[1]
                    for d in deps:
                        if d is None:
                            continue
                        if is_dma and d.sem == tok.sem and d.val < tok.val:
                            continue
                        if seen.get(d.sem, -1) >= d.val:
                            continue
                        eng.wait_ge(sems[d.sem], d.val)
                        seen[d.sem] = d.val
                    try:
                        ins = getattr(eng, op)(*args, **kwargs)
                    except Exception as e:
                        raise RuntimeError(f"emit failed: engine={key} op={op} kwargs={list(kwargs)}: {e}") from e
                    ins.then_inc(sems[tok.sem], 16 if is_dma else 1)
                for sk, v in cnts.items():
                    if sk.startswith(key + "D") and seen.get(sk, -1) < v:
                        eng.wait_ge(sems[sk], v)
            return emit
        for k in ("t", "v", "s", "g", "sy"):
            rec = getattr(px, k)
            if rec.items:
                getattr(block, engmap[k])(mk_emit(rec, k))
        block_h.__exit__(None, None, None)
        return self


def mm(P, K, M, N, lhsT_fn, rhs_fn, evac, psums, dep=None, kchunk=128,
       mchunk=128, nchunk=512):
    nk = (K + kchunk - 1) // kchunk
    toks = []
    last_use = {}
    pi = 0
    for mc in range((M + mchunk - 1) // mchunk):
        for ncc in range((N + nchunk - 1) // nchunk):
            ps = psums[pi % len(psums)][:, :]
            war = last_use.get(pi % len(psums))
            mt = None
            for kc in range(nk):
                d = []
                if dep is not None:
                    d += dep if isinstance(dep, list) else [dep]
                if war is not None:
                    d.append(war)
                mt = P.t.matmul(ps, lhsT_fn(kc, mc), rhs_fn(kc, ncc),
                                start=(kc == 0), stop=(kc == nk - 1), dep=d)
            et = evac(mc, ncc, ps, mt)
            last_use[pi % len(psums)] = et
            toks.append(et)
            pi += 1
    return toks


def to_bf(x):
    return np.ascontiguousarray(np.asarray(x).astype(NBF))
def f32c(x):
    return np.ascontiguousarray(np.asarray(x, dtype=np.float32))

IDENT128 = np.eye(128, dtype=NBF)

_PROGS = {}

def _run(key, buildfn, in_maps):
    if key not in _PROGS:
        _PROGS[key] = buildfn()
    res = run_bass_kernel_spmd(_PROGS[key].nc, in_maps, core_ids=list(range(NCORES)))
    return res.results



def build_l0_pre():
    pr = Prog("l0_pre")
    NT = TPC // 128
    x_in = pr.din("x", [TPC, D], F32)
    nw_in = pr.din("nw", [1, D], F32)
    nb_in = pr.din("nb", [1, D], F32)
    wq_in = pr.din("wq", [D, 512], F32)
    wkv_in = pr.din("wkv", [D, 128], F32)
    ws_in = pr.din("ws", [D, 8], F32)
    bs_in = pr.din("bs", [1, 8], F32)
    id_in = pr.din("ident", [128, 128], BF16)

    xn_o = pr.dout("xn_tok", [TPC, D], BF16)
    qT_o = pr.dout("qT", [512, TPC], BF16)
    kvT_o = pr.dout("kvT", [128, TPC], BF16)
    rp_o = pr.dout("rp", [TPC, 8], F32)

    xs = pr.sb("xs", [128, D], F32)
    xc = pr.sb("xc", [128, D], F32)
    sq = pr.sb("sq", [128, D], F32)
    xnb = pr.sb("xnb", [128, NT * D], BF16)
    xn2b = pr.sb("xn2b", [128, NT * D], BF16)
    mu = pr.sb("mu", [128, 1], F32)
    ss = pr.sb("ss", [128, 1], F32)
    rstd = pr.sb("rstd", [128, 1], F32)
    gb = pr.sb("gb", [128, D], F32)
    bb = pr.sb("bb", [128, D], F32)
    ident = pr.sb("identsb", [128, 128], BF16)
    ones1 = pr.sb("ones1", [1, 512], BF16)
    ones1f = pr.sb("ones1f", [1, 512], F32)
    wq = pr.sb("wq", [128, 8 * 512], BF16)
    wkv = pr.sb("wkv", [128, 8 * 128], BF16)
    ws = pr.sb("ws", [128, 8 * 8], BF16)
    bsb = pr.sb("bsb", [128, 8], F32)
    bsr = pr.sb("bsr", [1, 8], F32)
    xnT = pr.sb("xnT", [128, 8 * TPC], BF16)
    xn2T = pr.sb("xn2T", [128, 8 * TPC], BF16)
    qTs = pr.sb("qTs", [128, 4 * TPC], BF16)
    kvTs = pr.sb("kvTs", [128, TPC], BF16)
    rl = pr.sb("rl", [128, 8], F32)
    rmax = pr.sb("rmax", [128, 1], F32)
    rsum = pr.sb("rsum", [128, 1], F32)
    rrec = pr.sb("rrec", [128, 1], F32)
    rpt = pr.sb("rpt", [128, NT * 8], F32)

    pr.constv(1e-5)
    pst = pr.ps("pst", [128, 128], BF16)
    pst2 = pr.ps("pst2", [128, 128], BF16)
    pmm = [pr.ps(f"pmm{i}", [128, 512], F32) for i in range(4)]
    pbc = pr.ps("pbc", [128, 512], F32)
    prt = pr.ps("prt", [128, 8], F32)

    def body(P):
        ldw = P.g.dma_start(out=wq.rearrange("p (c o) -> p c o", c=8), in_=wq_in.rearrange("(c p) o -> p c o", p=128))
        ldkv = P.g.dma_start(out=wkv.rearrange("p (c o) -> p c o", c=8), in_=wkv_in.rearrange("(c p) o -> p c o", p=128))
        ldws = P.g.dma_start(out=ws.rearrange("p (c o) -> p c o", c=8), in_=ws_in.rearrange("(c p) o -> p c o", p=128))
        ldi = P.sy.dma_start(out=ident[:, :], in_=id_in[:, :])
        mo = P.v.memset(ones1[:, :], 1.0)
        mof = P.v.memset(ones1f[:, :], 1.0)
        ldbs = P.sy.dma_start(out=bsr[:, :], in_=bs_in[:, :])

        nwr = pr.sb("nwr", [1, D], F32)
        nbr = pr.sb("nbr", [1, D], F32)
        l1 = P.sy.dma_start(out=nwr[:, :], in_=nw_in[:, :])
        l2 = P.sy.dma_start(out=nbr[:, :], in_=nb_in[:, :])
        bt = []
        for half in range(2):
            slc = slice(half * 512, (half + 1) * 512)
            m1 = P.t.matmul(pbc[:, :], ones1f[0:1, 0:128], nwr[0:1, slc], start=True, stop=True, dep=[l1, mof, (bt[-1] if bt else None)])
            bt.append(P.v.tensor_copy(gb[:, slc], pbc[:, :], dep=m1))
            m2 = P.t.matmul(pbc[:, :], ones1f[0:1, 0:128], nbr[0:1, slc], start=True, stop=True, dep=[l2, bt[-1]])
            bt.append(P.v.tensor_copy(bb[:, slc], pbc[:, :], dep=m2))
        m3 = P.t.matmul(prt[:, :], ones1f[0:1, 0:128], bsr[0:1, 0:8], start=True, stop=True, dep=[ldbs, mof, bt[-1]])
        bsb_t = P.v.tensor_copy(bsb[:, :], prt[:, :], dep=m3)

        ln_toks = []
        prev = None
        for i in range(NT):
            ld = P.sy.dma_start(out=xs[:, :], in_=x_in[i * 128:(i + 1) * 128, :], dep=prev)
            r = P.v.tensor_reduce(mu[:, :], xs[:, :], axis=AX.X, op=ALU.add, dep=ld)
            r = P.v.tensor_scalar(mu[:, :], mu[:, :], 1.0 / D, None, op0=ALU.mult, dep=r)
            r = P.v.tensor_scalar(xc[:, :], xs[:, :], mu[:, 0:1], None, op0=ALU.subtract, dep=r)
            r2 = P.s.activation(sq[:, :], xc[:, :], AF.Square, accum_out=ss[:, 0:1], dep=r)
            r3 = P.s.activation(rstd[:, :], ss[:, :], AF.Sqrt, bias=1e-5, scale=1.0 / D, dep=r2)
            r4 = P.v.reciprocal(rstd[:, :], rstd[:, :], dep=r3)
            r5 = P.v.tensor_scalar(xs[:, :], xc[:, :], rstd[:, 0:1], None, op0=ALU.mult, dep=r4)
            rb = P.v.tensor_copy(xnb[:, i * D:(i + 1) * D], xs[:, :], dep=r5)
            q = P.v.tensor_reduce(mu[:, :], xs[:, :], axis=AX.X, op=ALU.add, dep=r5)
            q = P.v.tensor_scalar(mu[:, :], mu[:, :], 1.0 / D, None, op0=ALU.mult, dep=q)
            q = P.v.tensor_scalar(xc[:, :], xs[:, :], mu[:, 0:1], None, op0=ALU.subtract, dep=q)
            q2 = P.s.activation(sq[:, :], xc[:, :], AF.Square, accum_out=ss[:, 0:1], dep=q)
            q3 = P.s.activation(rstd[:, :], ss[:, :], AF.Sqrt, bias=1e-5, scale=1.0 / D, dep=q2)
            q4 = P.v.reciprocal(rstd[:, :], rstd[:, :], dep=q3)
            q5 = P.v.tensor_scalar(xc[:, :], xc[:, :], rstd[:, 0:1], None, op0=ALU.mult, dep=q4)
            q6 = P.v.tensor_mul(xc[:, :], xc[:, :], gb[:, :], dep=[q5, bt[-1]])
            q7 = P.v.tensor_add(xc[:, :], xc[:, :], bb[:, :], dep=q6)
            q8 = P.v.tensor_copy(xn2b[:, i * D:(i + 1) * D], xc[:, :], dep=q7)
            prev = q8
            ln_toks.append((rb, q8))
        st_xn = P.sy.dma_start(out=xn_o.rearrange("(n p) d -> p n d", p=128),
                               in_=xnb.rearrange("p (n d) -> p n d", d=D), dep=[a for a, _ in ln_toks])

        tr_toks = []
        prev_ev = [None, None]
        for f in range(8):
            for i in range(NT):
                d1 = [ldi, ln_toks[i][0]]
                tp = P.t.transpose(pst[:, :], xnb[:, i * D + f * 128:i * D + (f + 1) * 128], ident[:, :],
                                   dep=d1 + ([prev_ev[0]] if prev_ev[0] else []))
                prev_ev[0] = P.v.tensor_copy(xnT[:, f * TPC + i * 128: f * TPC + (i + 1) * 128], pst[:, :], dep=tp)
                tp2 = P.t.transpose(pst2[:, :], xn2b[:, i * D + f * 128:i * D + (f + 1) * 128], ident[:, :],
                                    dep=[ldi, ln_toks[i][1]] + ([prev_ev[1]] if prev_ev[1] else []))
                prev_ev[1] = P.v.tensor_copy(xn2T[:, f * TPC + i * 128: f * TPC + (i + 1) * 128], pst2[:, :], dep=tp2)
            tr_toks.append((prev_ev[0], prev_ev[1]))
        trdep = [tr_toks[-1][0], tr_toks[-1][1]]

        def lq(kc, mc):
            return wq[:, kc * 512 + mc * 128: kc * 512 + (mc + 1) * 128]
        def rq(kc, ncc):
            return xn2T[:, kc * TPC + ncc * 512: kc * TPC + ncc * 512 + 512]
        def evq(mc, ncc, ps_, tk):
            return P.s.activation(qTs[:, mc * TPC + ncc * 512: mc * TPC + ncc * 512 + 512],
                                  ps_, AF.Copy, scale=0.125, dep=tk)
        t_q = mm(P, D, 512, TPC, lq, rq, evq, pmm[0:2], dep=[ldw] + trdep)
        st_q = P.sy.dma_start(out=qT_o.rearrange("(c p) t -> p c t", p=128), in_=qTs.rearrange("p (c t) -> p c t", c=4), dep=t_q)

        def lkv(kc, mc):
            return wkv[:, kc * 128:(kc + 1) * 128]
        def rkv(kc, ncc):
            return xn2T[:, kc * TPC + ncc * 512: kc * TPC + ncc * 512 + 512]
        def evkv(mc, ncc, ps_, tk):
            return P.s.activation(kvTs[:, ncc * 512:(ncc + 1) * 512], ps_, AF.Copy, dep=tk)
        t_kv = mm(P, D, 128, TPC, lkv, rkv, evkv, pmm[2:4], dep=[ldkv] + trdep)
        st_kv = P.sy.dma_start(out=kvT_o[:, :], in_=kvTs[:, :], dep=t_kv)

        rp_toks = []
        for i in range(NT):
            mt = None
            for kc in range(8):
                mt = P.t.matmul(prt[:, :], xnT[:, kc * TPC + i * 128: kc * TPC + (i + 1) * 128],
                                ws[:, kc * 8:(kc + 1) * 8], start=(kc == 0), stop=(kc == 7),
                                dep=[ldws] + trdep + ([rp_toks[-1]] if rp_toks else [bsb_t]))
            a1 = P.v.tensor_add(rl[:, :], prt[:, :], bsb[:, :], dep=mt)
            a2 = P.v.tensor_reduce(rmax[:, :], rl[:, :], axis=AX.X, op=ALU.max, negate=True, dep=a1)
            a3 = P.s.activation(rl[:, :], rl[:, :], AF.Exp, bias=rmax[:, 0:1], accum_out=rsum[:, 0:1], dep=a2)
            a4 = P.v.reciprocal(rrec[:, :], rsum[:, :], dep=a3)
            a5 = P.v.tensor_scalar(rpt[:, i * 8:(i + 1) * 8], rl[:, :], rrec[:, 0:1], None, op0=ALU.mult, dep=a4)
            rp_toks.append(a5)
        st_rp = P.sy.dma_start(out=rp_o.rearrange("(n p) e -> p n e", p=128), in_=rpt.rearrange("p (n e) -> p n e", e=8), dep=rp_toks)

        P.sy.nop(dep=[st_xn, st_q, st_kv, st_rp])

    return pr.build(body)


def prep_l0_pre(st):
    maps = []
    for c in range(NCORES):
        maps.append({
            "x": f32c(st["x"][c * TPC:(c + 1) * TPC]),
            "nw": f32c(st["in"]["L0_attn_norm_w"]).reshape(1, D),
            "nb": f32c(st["in"]["L0_attn_norm_b"]).reshape(1, D),
            "wq": f32c(st["in"]["L0_attn_Wq"]),
            "wkv": f32c(st["in"]["L0_attn_Wkv"]),
            "ws": f32c(st["in"]["L0_ffn_Ws"]),
            "bs": f32c(st["in"]["L0_ffn_bs"]).reshape(1, 8),
            "ident": IDENT128,
        })
    return maps


def post_l0_pre(st, res):
    st["xn0_tok"] = np.concatenate([r["xn_tok"] for r in res], 0)
    st["l0_qT"] = np.concatenate([r["qT"] for r in res], 1)
    kv = np.concatenate([r["kvT"] for r in res], 1)
    st["l0_kT"] = kv[:64]
    st["l0_vT"] = kv[64:]
    st["l0_rp"] = np.concatenate([r["rp"] for r in res], 0)



def kernel(**inputs):
    st = {"in": inputs, "x": f32c(inputs["L0_attn_x"]).reshape(N_TOK, D)}
    for key in ("l0_pre", "l0_attn", "l0_moe", "l0_fin", "l1_pre", "l1_scan",
                "l1_out", "l2_pre", "l2_delta", "l2_mix"):
        res = _run(key, globals()["build_" + key], globals()["prep_" + key](st))
        globals()["post_" + key](st, res)
    st["l2_yT"] = st["l2_xn2T"]
    for li in range(6):
        for key in ("l2fa", "l2fb", "l2ff"):
            res = _run(key, globals()["build_" + key], globals()["prep_" + key](st, li))
            globals()["post_" + key](st, res)
    for key in ("l3_pre", "l3_attn", "l3_out"):
        res = _run(key, globals()["build_" + key], globals()["prep_" + key](st))
        globals()["post_" + key](st, res)
    return st["final"]



def build_l0_attn():
    pr = Prog("l0_attn")
    L = 4096
    q_in = pr.din("qh", [256, L], BF16)
    k_in = pr.din("kt", [64, L], BF16)
    v_in = pr.din("vt", [64, L], BF16)
    id_in = pr.din("ident", [128, 128], BF16)
    av_o = pr.dout("avT", [256, L], BF16)

    qh = pr.sb("qh", [64, 4 * L], BF16)
    kt = pr.sb("kt", [64, L], BF16)
    vt = pr.sb("vt", [64, L], BF16)
    v65 = pr.sb("v65", [128, 32 * 65], BF16)
    ident = pr.sb("ident", [128, 128], BF16)
    ones1f = pr.sb("ones1f", [1, 128], F32)
    sqs = pr.sb("sqs", [64, L], BF16)
    onesb = pr.sb("onesb", [64, 1], BF16)
    nsum = pr.sb("nsum", [1, 512], F32)
    mxk = pr.sb("mxk", [1, 1], F32)
    mxq = pr.sb("mxq", [1, 1], F32)
    cneg = pr.sb("cneg", [1, 1], F32)
    negcb = pr.sb("negcb", [128, 1], F32)
    ebuf = [pr.sb(f"ebuf{i}", [128, 512], BF16) for i in range(2)]
    avsb = pr.sb("avsb", [65, 512], F32)
    rc = pr.sb("rc", [1, 512], F32)
    avout = pr.sb("avout", [64, L], BF16)

    pst = pr.ps("pst", [128, 64], BF16)
    psA = [pr.ps(f"psA{i}", [128, 512], F32) for i in range(2)]
    psD = pr.ps("psD", [65, 512], F32)
    psB = pr.ps("psB", [64, 512], F32)
    psS = pr.ps("psS", [1, 512], F32)
    psN = pr.ps("psN", [128, 1], F32)

    def body(P):
        lq = P.sy.dma_start(out=qh.rearrange("p (h l) -> p h l", h=4),
                            in_=q_in.rearrange("(h p) l -> p h l", p=64))
        lk = P.sy.dma_start(out=kt[:, :], in_=k_in[:, :])
        lv = P.sy.dma_start(out=vt[:, :], in_=v_in[:, :])
        li = P.sy.dma_start(out=ident[:, :], in_=id_in[:, :])
        o1 = P.v.memset(ones1f[:, :], 1.0)
        o2 = P.v.memset(onesb[:, :], 1.0)

        prev_ev = None
        for jc in range(32):
            tp = P.t.transpose(pst[:, :], vt[:, jc * 128:(jc + 1) * 128], ident[0:64, 0:64],
                               dep=[lv, li] + ([prev_ev] if prev_ev else []))
            cv = P.v.tensor_copy(v65[:, jc * 65: jc * 65 + 64], pst[:, 0:64], dep=tp)
            prev_ev = P.v.memset(v65[:, jc * 65 + 64: jc * 65 + 65], 1.0, dep=cv)
        v65done = prev_ev

        def colmax_sq(src_ap, out11, dep):
            sq = P.s.activation(sqs[:, :], src_ap, AF.Square, dep=dep)
            t = None
            for tc in range(8):
                m1 = P.t.matmul(psS[:, :], onesb[:, 0:1], sqs[:, tc * 512:(tc + 1) * 512],
                                start=True, stop=True, dep=[sq, o2] + ([t] if t else []))
                c1 = P.v.tensor_copy(nsum[:, :], psS[:, :], dep=m1)
                r1 = P.v.tensor_reduce(nsum[:, 0:1], nsum[:, :], axis=AX.X, op=ALU.max, dep=c1)
                if tc == 0:
                    t = P.v.tensor_copy(out11[:, :], nsum[:, 0:1], dep=r1)
                else:
                    t = P.v.tensor_max(out11[:, :], out11[:, :], nsum[:, 0:1], dep=r1)
            return t

        tk = colmax_sq(kt[:, :], mxk, [lk])

        st_toks = []
        war_psD = []
        war_psB = []
        war_avsb = []
        for h in range(4):
            qsl = qh[:, h * L:(h + 1) * L]
            tq = colmax_sq(qsl, mxq, [lq, tk] + st_toks[-1:])
            c0 = P.v.tensor_mul(cneg[:, :], mxq[:, :], mxk[:, :], dep=[tq, tk])
            c1 = P.s.activation(cneg[:, :], cneg[:, :], AF.Sqrt, dep=c0)
            c2 = P.v.tensor_scalar(cneg[:, :], cneg[:, :], -1.0, None, op0=ALU.mult, dep=c1)
            c3 = P.t.matmul(psN[:, :], ones1f[0:1, :], cneg[0:1, 0:1], start=True, stop=True,
                            dep=[c2, o1] + st_toks[-1:])
            c4 = P.v.tensor_copy(negcb[:, :], psN[:, :], dep=c3)

            for ic in range(8):
                icol = slice(ic * 512, (ic + 1) * 512)
                edone = [None, None]
                dmm = None
                for jc in range(32):
                    pa = psA[jc % 2]
                    m1 = P.t.matmul(pa[:, :], kt[:, jc * 128:(jc + 1) * 128],
                                    qh[:, h * L + ic * 512:h * L + ic * 512 + 512],
                                    start=True, stop=True,
                                    dep=[lq, lk, c4] + ([edone[jc % 2]] if edone[jc % 2] else []))
                    ex = P.s.activation(ebuf[jc % 2][:, :], pa[:, :], AF.Exp, bias=negcb[:, 0:1],
                                        dep=[m1] + ([dmm] if dmm else []))
                    dmm = P.t.matmul(psD[:, :], v65[:, jc * 65: jc * 65 + 65], ebuf[jc % 2][:, :],
                                     start=(jc == 0), stop=(jc == 31),
                                     dep=[ex, v65done] + (war_psD if jc == 0 else []))
                    edone[jc % 2] = dmm
                ev = P.s.copy(avsb[:, :], psD[:, :], dep=[dmm] + war_avsb)
                war_psD = [ev]
                war_avsb = []
                r0 = P.v.reciprocal(rc[:, :], avsb[64:65, :], dep=ev)
                b1 = P.t.matmul(psB[:, :], ones1f[0:1, 0:64], rc[0:1, :], start=True, stop=True,
                                dep=[r0] + war_psB)
                mlt = P.v.tensor_tensor(avout[:, icol], avsb[0:64, :], psB[:, :], ALU.mult, dep=b1)
                war_psB = [mlt]
                war_avsb = [mlt]
                st_toks.append(mlt)
            st_toks.append(P.sy.dma_start(out=av_o[h * 64:(h + 1) * 64, :], in_=avout[:, :],
                                          dep=st_toks[-8:]))
        P.sy.nop(dep=st_toks[-1])

    return pr.build(body)


def prep_l0_attn(st):
    qT = st["l0_qT"]
    kT = st["l0_kT"]
    vT = st["l0_vT"]
    maps = []
    for c in range(NCORES):
        b = c // 2
        hs = 4 * (c % 2)
        cols = slice(b * 4096, (b + 1) * 4096)
        qrows = np.concatenate([qT[(hs + i) * 64:(hs + i + 1) * 64, cols] for i in range(4)], 0)
        maps.append({
            "qh": np.ascontiguousarray(qrows),
            "kt": np.ascontiguousarray(kT[:, cols]),
            "vt": np.ascontiguousarray(vT[:, cols]),
            "ident": IDENT128,
        })
    return maps


def post_l0_attn(st, res):
    avT = np.zeros((512, N_TOK), NBF)
    for c in range(NCORES):
        b = c // 2
        hs = 4 * (c % 2)
        cols = slice(b * 4096, (b + 1) * 4096)
        for i in range(4):
            avT[(hs + i) * 64:(hs + i + 1) * 64, cols] = res[c]["avT"][i * 64:(i + 1) * 64]
    st["l0_avT"] = avT


def build_l0_fin():
    pr = Prog("l0_fin")
    NT = TPC // 128
    av_in = pr.din("av", [512, TPC], BF16)
    wo_in = pr.din("wo", [512, D], F32)
    onw_in = pr.din("onw", [1, D], F32)
    onb_in = pr.din("onb", [1, D], F32)
    x_in = pr.din("x", [TPC, D], F32)
    de_in = pr.din("dense", [8 * TPC, D], BF16)
    x1_o = pr.dout("x1", [TPC, D], F32)
    pr.constv(1e-5)

    av = pr.sb("av", [128, 4 * TPC], BF16)
    wo = pr.sb("wo", [128, 4 * D], BF16)
    ones1f = pr.sb("ones1f", [1, 512], F32)
    wr = pr.sb("wr", [1, D], F32)
    br = pr.sb("br", [1, D], F32)
    gb = pr.sb("gb", [128, D], F32)
    bb = pr.sb("bb", [128, D], F32)
    ao = pr.sb("ao", [128, D], F32)
    xc = pr.sb("xc", [128, D], F32)
    sq = pr.sb("sq", [128, D], F32)
    mu = pr.sb("mu", [128, 1], F32)
    ss = pr.sb("ss", [128, 1], F32)
    rstd = pr.sb("rstd", [128, 1], F32)
    acc = pr.sb("acc", [128, D], F32)
    dsl = pr.sb("dsl", [128, D], BF16)
    pbc = pr.ps("pbc", [128, 512], F32)
    pm = [pr.ps(f"pm{i}", [128, 512], F32) for i in range(2)]

    def body(P):
        lav = P.sy.dma_start(out=av.rearrange("p (c t) -> p c t", c=4),
                             in_=av_in.rearrange("(c p) t -> p c t", p=128))
        lwo = P.g.dma_start(out=wo.rearrange("p (c o) -> p c o", c=4),
                            in_=wo_in.rearrange("(c p) o -> p c o", p=128))
        l1 = P.sy.dma_start(out=wr[:, :], in_=onw_in[:, :])
        l2 = P.sy.dma_start(out=br[:, :], in_=onb_in[:, :])
        mof = P.v.memset(ones1f[:, :], 1.0)
        bt = None
        for half in range(2):
            slc = slice(half * 512, (half + 1) * 512)
            m1 = P.t.matmul(pbc[:, :], ones1f[0:1, 0:128], wr[0:1, slc], start=True, stop=True,
                            dep=[l1, mof, bt])
            bt = P.v.tensor_copy(gb[:, slc], pbc[:, :], dep=m1)
            m2 = P.t.matmul(pbc[:, :], ones1f[0:1, 0:128], br[0:1, slc], start=True, stop=True, dep=[l2, bt])
            bt = P.v.tensor_copy(bb[:, slc], pbc[:, :], dep=m2)

        done = []
        for i in range(NT):
            evs = []
            for ncc in range(2):
                pp = pm[ncc]
                mt = None
                for kc in range(4):
                    mt = P.t.matmul(pp[:, :], av[:, kc * TPC + i * 128: kc * TPC + (i + 1) * 128],
                                    wo[:, kc * D + ncc * 512: kc * D + ncc * 512 + 512],
                                    start=(kc == 0), stop=(kc == 3),
                                    dep=[lav, lwo] + (done[-1:] if done else [bt]))
                evs.append(P.s.copy(ao[:, ncc * 512:(ncc + 1) * 512], pp[:, :], dep=mt))
            r = P.v.tensor_reduce(mu[:, :], ao[:, :], axis=AX.X, op=ALU.add, dep=evs)
            r = P.v.tensor_scalar(mu[:, :], mu[:, :], 1.0 / D, None, op0=ALU.mult, dep=r)
            r = P.v.tensor_scalar(xc[:, :], ao[:, :], mu[:, 0:1], None, op0=ALU.subtract, dep=r)
            r2 = P.s.activation(sq[:, :], xc[:, :], AF.Square, accum_out=ss[:, 0:1], dep=r)
            r3 = P.s.activation(rstd[:, :], ss[:, :], AF.Sqrt, bias=1e-5, scale=1.0 / D, dep=r2)
            r4 = P.v.reciprocal(rstd[:, :], rstd[:, :], dep=r3)
            r5 = P.v.tensor_scalar(xc[:, :], xc[:, :], rstd[:, 0:1], None, op0=ALU.mult, dep=r4)
            r6 = P.v.tensor_mul(xc[:, :], xc[:, :], gb[:, :], dep=r5)
            r7 = P.v.tensor_add(xc[:, :], xc[:, :], bb[:, :], dep=r6)
            lx = P.sy.dma_start(out=acc[:, :], in_=x_in[i * 128:(i + 1) * 128, :],
                                dep=done[-1:] if done else None)
            a0 = P.v.tensor_add(acc[:, :], acc[:, :], xc[:, :], dep=[r7, lx])
            for e in range(8):
                ld = P.g.dma_start(out=dsl[:, :], in_=de_in[e * TPC + i * 128: e * TPC + (i + 1) * 128, :],
                                   dep=a0 if e == 0 else a0e)
                a0e = P.v.tensor_add(acc[:, :], acc[:, :], dsl[:, :], dep=[ld, a0 if e == 0 else a0e])
            stx = P.sy.dma_start(out=x1_o[i * 128:(i + 1) * 128, :], in_=acc[:, :], dep=a0e)
            done.append(stx)
        P.sy.nop(dep=done[-1])

    return pr.build(body)


def prep_l0_fin(st):
    ins = st["in"]
    maps = []
    for c in range(NCORES):
        sl = slice(c * TPC, (c + 1) * TPC)
        dense = np.concatenate([st["l0_dense"][e][sl] for e in range(8)], 0)
        maps.append({
            "av": np.ascontiguousarray(st["l0_avT"][:, sl]),
            "wo": f32c(ins["L0_attn_Wo"]),
            "onw": f32c(ins["L0_attn_out_norm_w"]).reshape(1, D),
            "onb": f32c(ins["L0_attn_out_norm_b"]).reshape(1, D),
            "x": f32c(st["x"][sl]),
            "dense": np.ascontiguousarray(dense),
        })
    return maps


def post_l0_fin(st, res):
    st["x1"] = np.concatenate([r["x1"] for r in res], 0)



def build_l0_moe():
    pr = Prog("l0_moe")
    KSEL = 2048
    pcf_in = pr.din("pcf", [128, 128], F32)
    pci_in = pr.din("pci", [128, 128], I32)
    bits_in = pr.din("bits", [128, 31], I32)
    triu_in = pr.din("triu", [128, 128], F32)
    xn_in = pr.din("xn", [N_TOK, D], BF16)
    w1_in = pr.din("w1", [D, 4096], F32)
    b1_in = pr.din("b1c", [128, 32], F32)
    w2_in = pr.din("w2", [4096, D], F32)
    b2_in = pr.din("b2", [1, D], F32)
    id_in = pr.din("ident", [128, 128], BF16)
    de_o = pr.dout("dense", [N_TOK, D], BF16)
    cmpvt = pr.nc.dram_tensor("cmpvt", [KSEL + 1, 2], F32, kind="Internal").ap()

    pcf = pr.sb("pcf", [128, 128], F32)
    pci = pr.sb("pci", [128, 128], I32)
    bits = pr.sb("bits", [128, 31], I32)
    triu = pr.sb("triu", [128, 128], F32)
    ident = pr.sb("ident", [128, 128], BF16)
    ones1f = pr.sb("ones1f", [1, 128], F32)
    onescf = pr.sb("onescf", [128, 1], F32)
    th = pr.sb("th", [128, 1], I32)
    tht = pr.sb("tht", [128, 1], I32)
    cmpf = pr.sb("cmpf", [128, 128], F32)
    cnt1 = pr.sb("cnt1", [1, 128], F32)
    cnts = pr.sb("cnts", [1, 1], F32)
    cntb = pr.sb("cntb", [128, 1], F32)
    keepf = pr.sb("keepf", [128, 1], F32)
    keepi = pr.sb("keepi", [128, 1], I32)
    maskf = pr.sb("maskf", [128, 128], F32)
    maskb = pr.sb("maskb", [128, 128], BF16)
    mrow = pr.sb("mrow", [1, 128], F32)
    prow = [pr.sb(f"prow{i}", [1, 128], F32) for i in range(2)]
    posg = pr.sb("posg", [128, 128], F32)
    offs = pr.sb("offs", [128, 128], F32)
    offsi = pr.sb("offsi", [128, 128], I32)
    tokf = pr.sb("tokf", [128, 128], F32)
    toki = pr.sb("toki", [128, 128], I32)
    pairs = pr.sb("pairs", [128, 2], F32)
    zero2 = pr.sb("zero2", [128, 2048], BF16)
    xtile = pr.sb("xtile", [128, D], BF16)
    xinT = pr.sb("xinT", [128, 8 * KSEL], BF16)
    hT = pr.sb("hT", [128, 32 * 256], BF16)
    b1c = pr.sb("b1c", [128, 32], F32)
    b2r = pr.sb("b2r", [1, D], F32)
    b2b = pr.sb("b2b", [128, D], F32)
    b2c = pr.sb("b2c", [128, 8], F32)
    w1 = pr.sb("w1", [128, 8 * 4096], BF16)
    w2 = pr.sb("w2", [128, 32 * D], BF16)
    tt_ = pr.sb("tt", [128, 32], F32)
    tti = pr.sb("tti", [128, 1], I32)
    ysb = pr.sb("ysb", [128, D], BF16)
    yf = pr.sb("yf", [128, D], F32)

    psS = pr.ps("psS", [1, 128], F32)
    psN = pr.ps("psN", [128, 1], F32)
    psP = pr.ps("psP", [128, 128], F32)
    pst = pr.ps("pst", [128, 128], BF16)
    pm = [pr.ps(f"pm{i}", [128, 512], F32) for i in range(4)]

    def body(P):
        l1 = P.sy.dma_start(out=pcf[:, :], in_=pcf_in[:, :])
        l2 = P.sy.dma_start(out=pci[:, :], in_=pci_in[:, :])
        l3 = P.sy.dma_start(out=bits[:, :], in_=bits_in[:, :])
        l4 = P.sy.dma_start(out=triu[:, :], in_=triu_in[:, :])
        l5 = P.sy.dma_start(out=ident[:, :], in_=id_in[:, :])
        lb1 = P.sy.dma_start(out=b1c[:, :], in_=b1_in[:, :])
        lb2 = P.sy.dma_start(out=b2r[:, :], in_=b2_in[:, :])
        lw1 = P.g.dma_start(out=w1.rearrange("p (c o) -> p c o", c=8),
                            in_=w1_in.rearrange("(c p) o -> p c o", p=128))
        lw2 = P.g.dma_start(out=w2.rearrange("p (c o) -> p c o", c=32),
                            in_=w2_in.rearrange("(c p) o -> p c o", p=128))
        o1 = P.v.memset(ones1f[:, :], 1.0)
        o2 = P.v.memset(onescf[:, :], 1.0)
        zt = P.v.memset(zero2[:, :], 0.0)

        zts = []
        for i in range(64):
            zts.append(P.sy.dma_start(out=de_o.rearrange("(n p) d -> p n d", p=128)[:, i * 2:(i + 1) * 2, :],
                                      in_=zero2.rearrange("p (n d) -> p n d", d=D), dep=zt))

        t0 = P.v.memset(th[:, :], 0)
        prev = t0
        for bit in range(30, -1, -1):
            a1 = P.v.tensor_add(tht[:, :], th[:, :], bits[:, bit:bit + 1], dep=[prev, l3])
            a2 = P.v.tensor_tensor(cmpf[:, :], pci[:, :], tht[:, 0:1].to_broadcast([128, 128]), ALU.is_ge, dep=[a1, l2])
            m1 = P.t.matmul(psS[:, :], onescf[:, 0:1], cmpf[:, :], start=True, stop=True, dep=[a2, o2])
            c1 = P.v.tensor_copy(cnt1[:, :], psS[:, :], dep=m1)
            c2 = P.v.tensor_reduce(cnts[:, :], cnt1[:, :], axis=AX.X, op=ALU.add, dep=c1)
            m2 = P.t.matmul(psN[:, :], ones1f[0:1, :], cnts[0:1, 0:1], start=True, stop=True, dep=c2)
            c3 = P.v.tensor_copy(cntb[:, :], psN[:, :], dep=m2)
            k1 = P.v.tensor_scalar(keepf[:, :], cntb[:, :], 2048.0, None, op0=ALU.is_ge, dep=c3)
            k2 = P.v.tensor_tensor(keepi[:, :], keepf[:, :], bits[:, bit:bit + 1], ALU.mult, dep=k1)
            prev = P.v.tensor_add(th[:, :], th[:, :], keepi[:, :], dep=k2)

        mk = P.v.tensor_tensor(maskf[:, :], pci[:, :], th[:, 0:1].to_broadcast([128, 128]), ALU.is_ge, dep=prev)
        mc = P.t.matmul(psS[:, :], onescf[:, 0:1], maskf[:, :], start=True, stop=True, dep=mk)
        mr = P.v.tensor_copy(mrow[:, :], psS[:, :], dep=mc)
        s0 = P.v.memset(prow[0][:, 0:1], 0.0, dep=mr)
        s1 = P.v.tensor_copy(prow[0][:, 1:128], mrow[:, 0:127], dep=s0)
        cur = 0
        tprev = s1
        for sh in (1, 2, 4, 8, 16, 32, 64):
            nxt = 1 - cur
            tcopy = P.v.tensor_copy(prow[nxt][:, 0:sh], prow[cur][:, 0:sh], dep=tprev)
            tadd = P.v.tensor_add(prow[nxt][:, sh:128], prow[cur][:, sh:128], prow[cur][:, 0:128 - sh], dep=tcopy)
            cur = nxt
            tprev = tadd
        g1 = P.t.matmul(psP[:, :], triu[:, :], maskf[:, :], start=True, stop=False, dep=[mk, l4, tprev])
        g2 = P.t.matmul(psP[:, :], ones1f[0:1, 0:128], prow[cur][0:1, :], start=False, stop=True, dep=[g1, tprev, o1])
        pg = P.v.tensor_copy(posg[:, :], psP[:, :], dep=g2)
        f1 = P.v.tensor_scalar(offs[:, :], posg[:, :], -2048.0, None, op0=ALU.add, dep=pg)
        f2 = P.v.tensor_mul(offs[:, :], offs[:, :], maskf[:, :], dep=f1)
        f3 = P.v.tensor_scalar(offs[:, :], offs[:, :], 2048.0, None, op0=ALU.add, dep=f2)
        f4 = P.v.tensor_copy(offsi[:, :], offs[:, :], dep=f3)
        it1 = P.g.iota(toki[:, :], pattern=[[128, 128]], base=0, channel_multiplier=1)
        it2 = P.v.tensor_copy(tokf[:, :], toki[:, :], dep=it1)

        pv_toks = []
        prev_pair = None
        for j in range(128):
            p1 = P.v.tensor_copy(pairs[:, 0:1], pcf[:, j:j + 1], dep=[l1, prev_pair])
            p2 = P.v.tensor_copy(pairs[:, 1:2], tokf[:, j:j + 1], dep=[it2, p1])
            s = P.g.indirect_dma_start(
                out=cmpvt[:, :], out_offset=IndirectOffsetOnAxis(offsi[:, j:j + 1], 0),
                in_=pairs[:, :], in_offset=None, dep=[p2, f4])
            prev_pair = s
            pv_toks.append(s)
        lvt = P.sy.dma_start(out=tt_[:, 0:32].rearrange("p (n c) -> p n c", c=2),
                             in_=cmpvt[0:KSEL, :].rearrange("(n p) c -> p n c", p=128), dep=pv_toks)

        tr_toks = []
        prev_tr = None
        for i in range(16):
            ti0 = P.v.tensor_copy(tti[:, :], tt_[:, i * 2 + 1: i * 2 + 2],
                                  dep=[lvt] + ([prev_tr] if prev_tr else []))
            lx = P.g.indirect_dma_start(
                out=xtile[:, :], out_offset=None,
                in_=xn_in[:, :], in_offset=IndirectOffsetOnAxis(tti[:, 0:1], 0),
                dep=[ti0])
            for f in range(8):
                tp = P.t.transpose(pst[:, :], xtile[:, f * 128:(f + 1) * 128], ident[:, :],
                                   dep=[lx, l5] + ([prev_tr] if prev_tr else []))
                prev_tr = P.v.tensor_copy(xinT[:, f * KSEL + i * 128: f * KSEL + (i + 1) * 128],
                                          pst[:, :], dep=tp)
            tr_toks.append(prev_tr)

        bb1 = None
        for half in range(2):
            mmb = P.t.matmul(pm[3][:, :], ones1f[0:1, :], b2r[0:1, half * 512:(half + 1) * 512],
                             start=True, stop=True, dep=[lb2, o1, bb1])
            bb1 = P.v.tensor_copy(b2b[:, half * 512:(half + 1) * 512], pm[3][:, :], dep=mmb)

        st_toks = []
        for sc in range(8):
            ev_h = []
            for fc in range(32):
                pp = pm[fc % 2]
                mt = None
                for kc in range(8):
                    mt = P.t.matmul(pp[:, 0:256],
                                    w1[:, kc * 4096 + fc * 128: kc * 4096 + (fc + 1) * 128],
                                    xinT[:, kc * KSEL + sc * 256: kc * KSEL + sc * 256 + 256],
                                    start=(kc == 0), stop=(kc == 7),
                                    dep=[lw1] + tr_toks + (ev_h[-2:-1] if len(ev_h) >= 2 else []))
                ev_h.append(P.s.activation(hT[:, fc * 256:(fc + 1) * 256], pp[:, 0:256],
                                           AF.Gelu_apprx_tanh, bias=b1c[:, fc:fc + 1],
                                           dep=[mt, lb1]))
            for ssc in range(2):
                gs = sc * 2 + ssc
                evy = []
                for ncc in range(2):
                    pp = pm[2]
                    mt = None
                    for kc in range(32):
                        mt = P.t.matmul(pp[:, :],
                                        hT[:, kc * 256 + ssc * 128: kc * 256 + (ssc + 1) * 128],
                                        w2[:, kc * D + ncc * 512: kc * D + ncc * 512 + 512],
                                        start=(kc == 0), stop=(kc == 31),
                                        dep=[lw2] + ev_h + (evy[-1:] if evy else st_toks[-1:]))
                    evy.append(P.v.tensor_add(yf[:, ncc * 512:(ncc + 1) * 512], pp[:, :],
                                              b2b[:, ncc * 512:(ncc + 1) * 512], dep=[mt, bb1]))
                vcol = tt_[:, gs * 2: gs * 2 + 1]
                y1 = P.v.tensor_scalar(ysb[:, :], yf[:, :], vcol, None, op0=ALU.mult,
                                       dep=evy + [lvt])
                ti = P.v.tensor_copy(tti[:, :], tt_[:, gs * 2 + 1: gs * 2 + 2], dep=[lvt, y1])
                s = P.g.indirect_dma_start(
                    out=de_o[:, :], out_offset=IndirectOffsetOnAxis(tti[:, 0:1], 0),
                    in_=ysb[:, :], in_offset=None, dep=[y1, ti] + zts)
                st_toks.append(s)
        P.sy.nop(dep=st_toks[-1])

    return pr.build(body)


TRIU128 = np.triu(np.ones((128, 128), dtype=np.float32), 1)
BITS31 = np.broadcast_to((1 << np.arange(31, dtype=np.int32))[None, :], (128, 31)).copy()

def prep_l0_moe(st):
    ins = st["in"]
    xn = np.ascontiguousarray(st["xn0_tok"])
    maps = []
    for e in range(NCORES):
        pc = np.ascontiguousarray(st["l0_rp"][:, e].reshape(128, 128).T)
        maps.append({
            "pcf": pc,
            "pci": pc.view(np.int32),
            "bits": BITS31,
            "triu": TRIU128,
            "xn": xn,
            "w1": f32c(ins["L0_ffn_W1"][e]),
            "b1c": f32c(ins["L0_ffn_b1"][e]).reshape(32, 128).T.copy(),
            "w2": f32c(ins["L0_ffn_W2"][e]),
            "b2": f32c(ins["L0_ffn_b2"][e]).reshape(1, D),
            "ident": IDENT128,
        })
    return maps


def post_l0_moe(st, res):
    st["l0_dense"] = [res[e]["dense"] for e in range(NCORES)]


def build_l1_pre():
    pr = Prog("l1_pre")
    LT = 2176
    NTL = LT // 128
    x_in = pr.din("x", [LT, D], F32)
    ipw_in = pr.din("ipw", [D, 4096], F32)
    cw_in = pr.din("cw", [128, 16 * 4], F32)
    cb_in = pr.din("cb", [128, 16], F32)
    xpw_in = pr.din("xpw", [2048, 96], F32)
    dtw_in = pr.din("dtw", [64, 2048], F32)
    dtb_in = pr.din("dtb", [128, 16], F32)
    adj_in = pr.din("adj", [2048, 2048], F32)
    id_in = pr.din("ident", [128, 128], BF16)

    xn1T_o = pr.dout("xn1T", [D, TPC], BF16)
    xiT_o = pr.dout("xiT", [2048, TPC], BF16)
    resT_o = pr.dout("resT", [2048, TPC], BF16)
    dpT_o = pr.dout("dpT", [2048, TPC], F32)
    bct_o = pr.dout("bct", [32, TPC], F32)
    pr.constv(1e-5)
    pr.constv(1.0)

    xs = pr.sb("xs", [128, D], F32)
    xc = pr.sb("xc", [128, D], F32)
    sq = pr.sb("sq", [128, D], F32)
    mu = pr.sb("mu", [128, 1], F32)
    ss = pr.sb("ss", [128, 1], F32)
    rstd = pr.sb("rstd", [128, 1], F32)
    xnb = pr.sb("xnb", [128, D], BF16)
    xn1T = pr.sb("xn1T", [128, 8 * LT], BF16)
    ident = pr.sb("ident", [128, 128], BF16)
    wbuf = [pr.sb(f"wbuf{i}", [128, 8 * 128], BF16) for i in range(2)]
    xiT = pr.sb("xiT", [128, 16 * LT], BF16)
    resb = pr.sb("resb", [128, LT], BF16)
    cw = pr.sb("cw", [128, 16 * 4], F32)
    cb = pr.sb("cb", [128, 16], F32)
    acc = pr.sb("acc", [128, TPC], F32)
    xpw = pr.sb("xpw", [128, 16 * 96], BF16)
    xdT = pr.sb("xdT", [96, TPC], F32)
    xdTb = pr.sb("xdTb", [64, TPC], BF16)
    dtw = pr.sb("dtw", [64, 2048], BF16)
    dtb = pr.sb("dtb", [128, 16], F32)
    srl = pr.sb("srl", [128, 512], F32)
    sab = pr.sb("sab", [128, 512], F32)
    abuf = [pr.sb(f"abuf{i}", [128, 16 * 128], BF16) for i in range(2)]
    dpb = pr.sb("dpb", [128, 512], F32)

    pst = pr.ps("pst", [128, 128], BF16)
    pm = [pr.ps(f"pm{i}", [128, 512], F32) for i in range(4)]

    def body(P):
        li = P.sy.dma_start(out=ident[:, :], in_=id_in[:, :])
        lcw = P.sy.dma_start(out=cw[:, :], in_=cw_in[:, :])
        lcb = P.sy.dma_start(out=cb[:, :], in_=cb_in[:, :])
        lxp = P.g.dma_start(out=xpw.rearrange("p (c o) -> p c o", c=16),
                            in_=xpw_in.rearrange("(c p) o -> p c o", p=128))
        ldt = P.g.dma_start(out=dtw[:, :], in_=dtw_in[:, :])
        ldtb = P.sy.dma_start(out=dtb[:, :], in_=dtb_in[:, :])

        tr = None
        st_xn = []
        for i in range(NTL):
            ld = P.sy.dma_start(out=xs[:, :], in_=x_in[i * 128:(i + 1) * 128, :], dep=tr)
            r = P.v.tensor_reduce(mu[:, :], xs[:, :], axis=AX.X, op=ALU.add, dep=ld)
            r = P.v.tensor_scalar(mu[:, :], mu[:, :], 1.0 / D, None, op0=ALU.mult, dep=r)
            r = P.v.tensor_scalar(xc[:, :], xs[:, :], mu[:, 0:1], None, op0=ALU.subtract, dep=r)
            r2 = P.s.activation(sq[:, :], xc[:, :], AF.Square, accum_out=ss[:, 0:1], dep=r)
            r3 = P.s.activation(rstd[:, :], ss[:, :], AF.Sqrt, bias=1e-5, scale=1.0 / D, dep=r2)
            r4 = P.v.reciprocal(rstd[:, :], rstd[:, :], dep=r3)
            r5 = P.v.tensor_scalar(xnb[:, :], xc[:, :], rstd[:, 0:1], None, op0=ALU.mult, dep=r4)
            for f in range(8):
                tp = P.t.transpose(pst[:, :], xnb[:, f * 128:(f + 1) * 128], ident[:, :],
                                   dep=[li, r5] + ([tr] if tr else []))
                tr = P.v.tensor_copy(xn1T[:, f * LT + i * 128: f * LT + (i + 1) * 128], pst[:, :], dep=tp)
        st_xn = P.sy.dma_start(out=xn1T_o.rearrange("(c p) t -> p c t", p=128),
                               in_=xn1T.rearrange("p (c t) -> p c t", c=8)[:, :, 3:3 + TPC], dep=tr)

        NCH = [512, 512, 512, 512, 128]
        xr_done = []
        wld = [None, None]
        st_res = []
        for mc in range(32):
            wb = wbuf[mc % 2]
            lw = P.g.dma_start(out=wb.rearrange("p (c o) -> p c o", c=8),
                               in_=ipw_in.rearrange("(c p) o -> p c o", p=128)[:, :, mc * 128:(mc + 1) * 128],
                               dep=wld[mc % 2])
            col = 0
            evs = []
            for ci, cn in enumerate(NCH):
                pp = pm[ci % 2]
                mt = None
                for kc in range(8):
                    mt = P.t.matmul(pp[:, 0:cn], wb[:, kc * 128:(kc + 1) * 128],
                                    xn1T[:, kc * LT + col: kc * LT + col + cn],
                                    start=(kc == 0), stop=(kc == 7),
                                    dep=[lw, tr] + (evs[-2:-1] if len(evs) >= 2 else xr_done[-1:]))
                if mc < 16:
                    evs.append(P.s.activation(xiT[:, mc * LT + col: mc * LT + col + cn], pp[:, 0:cn],
                                              AF.Copy, dep=mt))
                else:
                    evs.append(P.s.activation(resb[:, col:col + cn], pp[:, 0:cn], AF.Copy,
                                              dep=[mt] + st_res[-1:]))
                col += cn
            if mc >= 16:
                st_res.append(P.sy.dma_start(out=resT_o[(mc - 16) * 128:(mc - 15) * 128, :],
                                             in_=resb[:, 3:3 + TPC], dep=evs))
            wld[mc % 2] = evs[-1]
            xr_done.append(evs[-1])

        conv_done = []
        st_xi = []
        for ct in range(16):
            base = ct * LT
            a1 = P.v.tensor_scalar(acc[:, :], xiT[:, base + 0: base + TPC], cw[:, ct * 4 + 0: ct * 4 + 1],
                                   None, op0=ALU.mult, dep=[xr_done[ct], lcw] + conv_done[-1:])
            for tap in range(1, 4):
                a1 = P.v.scalar_tensor_tensor(acc[:, :], xiT[:, base + tap: base + tap + TPC],
                                              cw[:, ct * 4 + tap: ct * 4 + tap + 1], acc[:, :],
                                              op0=ALU.mult, op1=ALU.add, dep=a1)
            a2 = P.v.tensor_scalar(acc[:, :], acc[:, :], cb[:, ct:ct + 1], None, op0=ALU.add, dep=[a1, lcb])
            a3 = P.s.activation(xiT[:, base + 3: base + 3 + TPC], acc[:, :], AF.Silu, dep=a2)
            conv_done.append(a3)
            st_xi.append(P.sy.dma_start(out=xiT_o[ct * 128:(ct + 1) * 128, :],
                                        in_=xiT[:, base + 3: base + 3 + TPC], dep=a3))

        xd_toks = []
        for ncc in range(4):
            pp = pm[2]
            mt = None
            for kc in range(16):
                mt = P.t.matmul(pp[0:96, :], xpw[:, kc * 96:(kc + 1) * 96],
                                xiT[:, kc * LT + 3 + ncc * 512: kc * LT + 3 + ncc * 512 + 512],
                                start=(kc == 0), stop=(kc == 15),
                                dep=[lxp] + conv_done + xd_toks[-1:])
            xd_toks.append(P.v.tensor_copy(xdT[:, ncc * 512:(ncc + 1) * 512], pp[0:96, :], dep=mt))
        st_bc = P.sy.dma_start(out=bct_o[:, :], in_=xdT[64:96, :], dep=xd_toks)
        cv = P.v.tensor_copy(xdTb[:, :], xdT[0:64, :], dep=xd_toks)

        del_toks = []
        for mc in range(16):
            for ncc in range(4):
                pp = pm[3]
                mt = P.t.matmul(pp[:, :], dtw[:, mc * 128:(mc + 1) * 128],
                                xdTb[:, ncc * 512:(ncc + 1) * 512],
                                start=True, stop=True,
                                dep=[ldt, cv] + del_toks[-1:])
                z1 = P.v.tensor_scalar(srl[:, :], pp[:, :], dtb[:, mc:mc + 1], 0.0,
                                       op0=ALU.add, op1=ALU.max, dep=[mt, ldtb])
                z2 = P.s.activation(sab[:, :], pp[:, :], AF.Abs, bias=dtb[:, mc:mc + 1], dep=[mt, z1])
                z3 = P.s.activation(sab[:, :], sab[:, :], AF.Exp, scale=-1.0, dep=z2)
                z4 = P.s.activation(sab[:, :], sab[:, :], AF.Ln, bias=1.0, dep=z3)
                z5 = P.v.tensor_add(xiT[:, mc * LT + ncc * 512: mc * LT + ncc * 512 + 512],
                                    srl[:, :], sab[:, :], dep=[z4] + st_xi)
                del_toks.append(z5)

        ald = [None, None]
        dp_toks = []
        for mc in range(16):
            ab = abuf[mc % 2]
            la = P.g.dma_start(out=ab.rearrange("p (c o) -> p c o", c=16),
                               in_=adj_in.rearrange("(c p) o -> p c o", p=128)[:, :, mc * 128:(mc + 1) * 128],
                               dep=ald[mc % 2])
            evs = []
            for ncc in range(4):
                pp = pm[ncc % 2]
                mt = None
                for kc in range(16):
                    mt = P.t.matmul(pp[:, :], ab[:, kc * 128:(kc + 1) * 128],
                                    xiT[:, kc * LT + ncc * 512: kc * LT + ncc * 512 + 512],
                                    start=(kc == 0), stop=(kc == 15),
                                    dep=[la] + del_toks[-1:] + (evs[-2:-1] if len(evs) >= 2 else dp_toks[-2:-1]))
                ev = P.v.tensor_copy(dpb[:, :], pp[:, :], dep=[mt] + dp_toks[-1:])
                evs.append(P.sy.dma_start(out=dpT_o[mc * 128:(mc + 1) * 128, ncc * 512:(ncc + 1) * 512],
                                          in_=dpb[:, :], dep=ev))
                dp_toks.append(evs[-1])
            ald[mc % 2] = evs[-1]
        P.sy.nop(dep=[st_xn, st_bc] + st_res[-1:] + st_xi[-1:] + dp_toks[-1:])

    return pr.build(body)


def prep_l1_pre(st):
    ins = st["in"]
    x1 = st["x1"]
    maps = []
    for c in range(NCORES):
        lo = c * TPC - 3
        xh = np.zeros((2176, D), np.float32)
        if c % 2 == 0:
            xh[3:3 + TPC] = x1[c * TPC:(c + 1) * TPC]
        else:
            xh[0:3 + TPC] = x1[lo:(c + 1) * TPC]
        cw = ins["L1_attn_conv_w"].reshape(2048, 4).reshape(16, 128, 4).transpose(1, 0, 2).reshape(128, 64)
        maps.append({
            "x": xh,
            "ipw": f32c(ins["L1_attn_in_proj_w"]),
            "cw": f32c(cw),
            "cb": f32c(ins["L1_attn_conv_b"].reshape(16, 128).T),
            "xpw": f32c(ins["L1_attn_x_proj_w"]),
            "dtw": f32c(ins["L1_attn_dt_proj_w"]),
            "dtb": f32c(ins["L1_attn_dt_proj_b"].reshape(16, 128).T),
            "adj": f32c(ins["L1_attn_adj"]),
            "ident": IDENT128,
        })
    return maps


def post_l1_pre(st, res):
    st["l1_xn1T"] = np.concatenate([r["xn1T"] for r in res], 1)
    st["l1_xiT"] = np.concatenate([r["xiT"] for r in res], 1)
    st["l1_resT"] = np.concatenate([r["resT"] for r in res], 1)
    st["l1_dpT"] = np.concatenate([r["dpT"] for r in res], 1)
    st["l1_bct"] = np.concatenate([r["bct"] for r in res], 1)


def build_l1_scan():
    pr = Prog("l1_scan")
    L = 4096
    H = 2048
    dp_in = pr.din("dpT", [2048, L], F32)
    xi_in = pr.din("xiT", [2048, L], BF16)
    bb_in = pr.din("bb", [8, L], F32)
    cb_in = pr.din("cbr", [8, L], F32)
    al_in = pr.din("alog", [128, 16 * 8], F32)
    dm_in = pr.din("dmul", [128, 16], F32)
    yp_o = pr.dout("ypT", [2048, L], BF16)

    dp = pr.sb("dp", [128, H], F32)
    xi = pr.sb("xi", [128, H], BF16)
    g = pr.sb("g", [128, H], BF16)
    a = pr.sb("a", [128, H], F32)
    u = pr.sb("u", [128, H], BF16)
    hb = pr.sb("hb", [128, H], BF16)
    hcarry = pr.sb("hcarry", [128, 8], F32)
    yac = pr.sb("yac", [128, H], F32)
    tmp = pr.sb("tmp", [128, H], BF16)
    bball = pr.sb("bball", [128, 8 * H], BF16)
    cball = pr.sb("cball", [128, 8 * H], BF16)
    am = pr.sb("am", [128, 16 * 8], F32)
    dm = pr.sb("dm", [128, 16], F32)

    def body(P):
        lal = P.sy.dma_start(out=am[:, :], in_=al_in[:, :])
        a1 = P.s.activation(am[:, :], am[:, :], AF.Exp, dep=lal)
        a2 = P.v.tensor_scalar(am[:, :], am[:, :], -1.0, None, op0=ALU.mult, dep=a1)
        ldm = P.sy.dma_start(out=dm[:, :], in_=dm_in[:, :])

        st_toks = []
        svmap = {}
        cvmap = {}
        ldmap = {}
        for half in range(2):
            tcol = slice(half * H, (half + 1) * H)
            lbbs = []
            for n_ in range(8):
                lbbs.append(P.g.dma_start(out=bball[:, n_ * H:(n_ + 1) * H],
                                          in_=bb_in[n_:n_ + 1, tcol].to_broadcast([128, H]),
                                          dep=st_toks[-1:]))
                lbbs.append(P.g.dma_start(out=cball[:, n_ * H:(n_ + 1) * H],
                                          in_=cb_in[n_:n_ + 1, tcol].to_broadcast([128, H]),
                                          dep=st_toks[-1:]))
            lbb = lbbs[-1]
            lcb = lbbs[-1]
            for ct in range(16):
                ldp = P.g.dma_start(out=dp[:, :], in_=dp_in[ct * 128:(ct + 1) * 128, tcol],
                                    dep=st_toks[-1:])
                lxi = P.g.dma_start(out=xi[:, :], in_=xi_in[ct * 128:(ct + 1) * 128, tcol],
                                    dep=st_toks[-1:])
                gg = P.v.tensor_tensor(g[:, :], dp[:, :], xi[:, :], ALU.mult, dep=[ldp, lxi])
                ya = None
                sc = None
                ym = None
                for n in range(8):
                    ex = P.s.activation(a[:, :], dp[:, :], AF.Exp, scale=am[:, ct * 8 + n: ct * 8 + n + 1],
                                        dep=[ldp, a2] + ([sc] if sc else st_toks[-1:]))
                    uu = P.v.tensor_tensor(u[:, :], g[:, :], bball[:, n * H:(n + 1) * H], ALU.mult,
                                           dep=[gg, lbb] + ([sc] if sc else []))
                    if half == 0:
                        sc = P.v.tensor_tensor_scan(hb[:, :], a[:, :], u[:, :], 0.0,
                                                    ALU.mult, ALU.add, dep=[ex, uu] + ([ym] if ym else []))
                        cvh = P.v.tensor_copy(hcarry[:, n:n + 1], hb[:, H - 1:H],
                                              dep=[sc] + ([svmap[ct]] if ct in svmap else []))
                        cvmap[(ct, n)] = cvh
                    else:
                        if n == 0:
                            ldmap[ct] = P.sy.dma_start(out=hcarry[:, :],
                                                       in_=hc_dram[ct * 128:(ct + 1) * 128, :],
                                                       dep=[svmap[ct]] + ([ym] if ym else []))
                        sc = P.v.tensor_tensor_scan(hb[:, :], a[:, :], u[:, :], hcarry[:, n:n + 1],
                                                    ALU.mult, ALU.add, dep=[ex, uu, ldmap[ct]] + ([ym] if ym else []))
                    ym = P.v.tensor_tensor(tmp[:, :], hb[:, :], cball[:, n * H:(n + 1) * H], ALU.mult,
                                           dep=[sc, lcb] + ([ya] if ya else []))
                    if n == 0:
                        ya = P.v.tensor_copy(yac[:, :], tmp[:, :], dep=ym)
                    else:
                        ya = P.v.tensor_add(yac[:, :], yac[:, :], tmp[:, :], dep=ym)
                if half == 0:
                    svmap[ct] = P.sy.dma_start(out=hc_dram[ct * 128:(ct + 1) * 128, :],
                                               in_=hcarry[:, :], dep=[cvmap[(ct, nn)] for nn in range(8)])
                yd = P.v.scalar_tensor_tensor(yac[:, :], xi[:, :], dm[:, ct:ct + 1], yac[:, :],
                                              op0=ALU.mult, op1=ALU.add, dep=[ya, ldm])
                st_toks.append(P.g.dma_start(out=yp_o[ct * 128:(ct + 1) * 128, tcol], in_=yac[:, :], dep=yd))
        P.sy.nop(dep=st_toks[-1])

    hc_dram = pr.nc.dram_tensor("hcarry_d", [2048, 8], F32, kind="Internal").ap()
    return pr.build(body)


def prep_l1_scan(st):
    ins = st["in"]
    maps = []
    for c in range(NCORES):
        b = c // 2
        n0 = (c % 2) * 8
        cols = slice(b * 4096, (b + 1) * 4096)
        bct = st["l1_bct"][:, cols]
        bb = f32c(bct[n0:n0 + 8])
        cbr = f32c(bct[16 + n0:16 + n0 + 8])
        alog = ins["L1_attn_A_log"][:, n0:n0 + 8]
        alr = f32c(alog.reshape(16, 128, 8).transpose(1, 0, 2).reshape(128, 128))
        dmul = (ins["L1_attn_D"].reshape(16, 128).T if c % 2 == 0 else np.zeros((128, 16), np.float32))
        maps.append({
            "dpT": f32c(st["l1_dpT"][:, cols]),
            "xiT": np.ascontiguousarray(st["l1_xiT"][:, cols]),
            "bb": bb,
            "cbr": cbr,
            "alog": alr,
            "dmul": f32c(dmul),
        })
    return maps


def post_l1_scan(st, res):
    st["l1_yp0"] = np.concatenate([res[2 * b]["ypT"] for b in range(4)], 1)
    st["l1_yp1"] = np.concatenate([res[2 * b + 1]["ypT"] for b in range(4)], 1)


def build_l1_out():
    pr = Prog("l1_out")
    NT = TPC // 128
    yp0_in = pr.din("yp0", [2048, TPC], BF16)
    yp1_in = pr.din("yp1", [2048, TPC], BF16)
    res_in = pr.din("resT", [2048, TPC], BF16)
    opw_in = pr.din("opw", [2048, D], F32)
    xn_in = pr.din("xn1T", [D, TPC], BF16)
    wg_in = pr.din("wg", [D, 8], F32)
    wfc_in = pr.din("wfc", [8 * D, 512], F32)
    wpr_in = pr.din("wpr", [8 * 512, D], F32)
    x1_in = pr.din("x1", [TPC, D], F32)
    x2_o = pr.dout("x2", [TPC, D], F32)

    y2 = pr.sb("y2", [128, TPC], BF16)
    ytl = pr.sb("ytl", [128, 16 * 128], BF16)
    sr = pr.sb("sr", [128, TPC], BF16)
    opw = pr.sb("opw", [128, 16 * D], BF16)
    xn1T = pr.sb("xn1T", [128, 8 * TPC], BF16)
    wg = pr.sb("wg", [128, 8 * 8], BF16)
    wfcb = [pr.sb(f"wfcb{i}", [128, 8 * 512], BF16) for i in range(2)]
    wprb = [pr.sb(f"wprb{i}", [128, 4 * D], BF16) for i in range(2)]
    xacc = pr.sb("xacc", [128, 16 * D], F32)
    rl = pr.sb("rl", [128, 8], F32)
    mx8 = pr.sb("mx8", [128, 8], F32)
    dd = pr.sb("dd", [128, 1], F32)
    w1c = pr.sb("w1c", [128, 1], F32)
    w2c = pr.sb("w2c", [128, 1], F32)
    msk = pr.sb("msk", [128, 8], F32)
    gts = pr.sb("gts", [128, 16 * 8], F32)
    hmid = pr.sb("hmid", [128, 4 * 128], BF16)
    macc = pr.sb("macc", [128, D], F32)
    tmp5 = pr.sb("tmp5", [128, 512], F32)

    pm = [pr.ps(f"pm{i}", [128, 512], F32) for i in range(4)]
    pr8 = pr.ps("pr8", [128, 8], F32)

    y2d = pr.nc.dram_tensor("y2d", [2048, TPC], BF16, kind="Internal").ap()

    def body(P):
        lr = P.sy.dma_start(out=xn1T.rearrange("p (c t) -> p c t", c=8),
                            in_=xn_in.rearrange("(c p) t -> p c t", p=128))
        lo = P.g.dma_start(out=opw.rearrange("p (c o) -> p c o", c=16),
                           in_=opw_in.rearrange("(c p) o -> p c o", p=128))
        lwg = P.g.dma_start(out=wg.rearrange("p (c o) -> p c o", c=8),
                            in_=wg_in.rearrange("(c p) o -> p c o", p=128))

        ych = []
        for ct in range(16):
            l0_ = P.g.dma_start(out=y2[:, :], in_=yp0_in[ct * 128:(ct + 1) * 128, :], dep=ych[-1:])
            l1_ = P.g.dma_start(out=sr[:, :], in_=yp1_in[ct * 128:(ct + 1) * 128, :], dep=ych[-1:])
            t1 = P.v.tensor_add(y2[:, :], y2[:, :], sr[:, :], dep=[l0_, l1_])
            l2_ = P.g.dma_start(out=sr[:, :], in_=res_in[ct * 128:(ct + 1) * 128, :], dep=t1)
            s1 = P.s.activation(sr[:, :], sr[:, :], AF.Silu, dep=l2_)
            m1 = P.v.tensor_mul(y2[:, :], y2[:, :], sr[:, :], dep=s1)
            ych.append(P.sy.dma_start(out=y2d[ct * 128:(ct + 1) * 128, :], in_=y2[:, :], dep=m1))

        ph2 = []
        for i in range(NT):
            ly = P.g.dma_start(out=ytl.rearrange("p (c t) -> p c t", c=16),
                               in_=y2d.rearrange("(c p) t -> p c t", p=128)[:, :, i * 128:(i + 1) * 128],
                               dep=ych + (ph2[-1:] if ph2 else []))
            evs = []
            for ncc in range(2):
                pp = pm[ncc]
                mt = None
                for kc in range(16):
                    mt = P.t.matmul(pp[:, :], ytl[:, kc * 128:(kc + 1) * 128],
                                    opw[:, kc * D + ncc * 512: kc * D + ncc * 512 + 512],
                                    start=(kc == 0), stop=(kc == 15),
                                    dep=[lo, ly] + (ph2[-1:] if ph2 else []))
                evs.append(P.s.copy(xacc[:, i * D + ncc * 512: i * D + ncc * 512 + 512], pp[:, :], dep=mt))
            lx = P.sy.dma_start(out=macc[:, :], in_=x1_in[i * 128:(i + 1) * 128, :],
                                dep=ph2[-1:] if ph2 else None)
            ad = P.v.tensor_add(xacc[:, i * D:(i + 1) * D], xacc[:, i * D:(i + 1) * D], macc[:, :],
                                dep=evs + [lx])
            mt = None
            for kc in range(8):
                mt = P.t.matmul(pr8[:, :], xn1T[:, kc * TPC + i * 128: kc * TPC + (i + 1) * 128],
                                wg[:, kc * 8:(kc + 1) * 8], start=(kc == 0), stop=(kc == 7),
                                dep=[lwg, lr] + (ph2[-1:] if ph2 else []))
            r0 = P.v.tensor_copy(rl[:, :], pr8[:, :], dep=[mt] + (ph2[-1:] if ph2 else []))
            m8 = P.v.max(out=mx8[:, :], in_=rl[:, :], dep=r0)
            d1 = P.v.tensor_sub(dd[:, :], mx8[:, 0:1], mx8[:, 1:2], dep=m8)
            s1 = P.s.activation(w1c[:, :], dd[:, :], AF.Sigmoid, dep=d1)
            s2 = P.v.tensor_scalar(w2c[:, :], w1c[:, :], -1.0, 1.0, op0=ALU.mult, op1=ALU.add, dep=s1)
            k1 = P.v.tensor_tensor(msk[:, :], rl[:, :], mx8[:, 0:1].to_broadcast([128, 8]), ALU.is_equal,
                                   dep=[m8, s2])
            g1 = P.v.tensor_scalar(gts[:, i * 8:(i + 1) * 8], msk[:, :], w1c[:, 0:1], None, op0=ALU.mult,
                                   dep=[k1, s1])
            k2 = P.v.tensor_tensor(msk[:, :], rl[:, :], mx8[:, 1:2].to_broadcast([128, 8]), ALU.is_equal, dep=g1)
            g2 = P.v.scalar_tensor_tensor(gts[:, i * 8:(i + 1) * 8], msk[:, :], w2c[:, 0:1],
                                          gts[:, i * 8:(i + 1) * 8], op0=ALU.mult, op1=ALU.add, dep=[k2, s2])
            ph2.append(P.v.tensor_copy(dd[:, :], w1c[:, :], dep=[g2, ad]))

        wld = [None, None]
        prev_h = None
        for e in range(8):
            lf = P.g.dma_start(out=wfcb[e % 2].rearrange("p (c o) -> p c o", c=8),
                               in_=wfc_in.rearrange("(c p) o -> p c o", p=128)[:, 8 * e:8 * (e + 1), :],
                               dep=wld[e % 2])
            lp = P.g.dma_start(out=wprb[e % 2].rearrange("p (c o) -> p c o", c=4),
                               in_=wpr_in.rearrange("(c p) o -> p c o", p=128)[:, 4 * e:4 * (e + 1), :],
                               dep=wld[e % 2])
            for i in range(NT):
                hts = []
                for ic in range(4):
                    pp = pm[2]
                    mt = None
                    for kc in range(8):
                        mt = P.t.matmul(pp[:, 0:128],
                                        wfcb[e % 2][:, kc * 512 + ic * 128: kc * 512 + (ic + 1) * 128],
                                        xn1T[:, kc * TPC + i * 128: kc * TPC + (i + 1) * 128],
                                        start=(kc == 0), stop=(kc == 7),
                                        dep=[lf, lr] + ph2 + (hts[-1:] if hts else ([prev_h] if prev_h else [])))
                    hts.append(P.s.activation(hmid[:, ic * 128:(ic + 1) * 128], pp[:, 0:128],
                                              AF.Gelu, dep=mt))
                for ncc in range(2):
                    pp = pm[3]
                    mt = None
                    for kc in range(4):
                        mt = P.t.matmul(pp[:, :], hmid[:, kc * 128:(kc + 1) * 128],
                                        wprb[e % 2][:, kc * D + ncc * 512: kc * D + ncc * 512 + 512],
                                        start=(kc == 0), stop=(kc == 3),
                                        dep=[lp] + hts + ([prev_h] if prev_h else []))
                    t5 = P.v.tensor_scalar(tmp5[:, :], pp[:, :], gts[:, i * 8 + e: i * 8 + e + 1], None,
                                           op0=ALU.mult, dep=[mt] + ([prev_h] if prev_h else []))
                    prev_h = P.v.tensor_add(xacc[:, i * D + ncc * 512: i * D + ncc * 512 + 512],
                                            xacc[:, i * D + ncc * 512: i * D + ncc * 512 + 512],
                                            tmp5[:, :], dep=[t5])
            wld[e % 2] = prev_h

        st_toks = []
        for i in range(NT):
            st_toks.append(P.sy.dma_start(out=x2_o[i * 128:(i + 1) * 128, :],
                                          in_=xacc[:, i * D:(i + 1) * D], dep=[prev_h]))
        P.sy.nop(dep=st_toks[-1])

    return pr.build(body)


def prep_l1_out(st):
    ins = st["in"]
    wfcT = np.ascontiguousarray(ins["L1_ffn_w_fc"].transpose(0, 2, 1)).reshape(8 * D, 512)
    wprT = np.ascontiguousarray(ins["L1_ffn_w_proj"].transpose(0, 2, 1)).reshape(8 * 512, D)
    maps = []
    for c in range(NCORES):
        sl = slice(c * TPC, (c + 1) * TPC)
        maps.append({
            "yp0": np.ascontiguousarray(st["l1_yp0"][:, sl]),
            "yp1": np.ascontiguousarray(st["l1_yp1"][:, sl]),
            "resT": np.ascontiguousarray(st["l1_resT"][:, sl]),
            "opw": f32c(ins["L1_attn_out_proj_w"]),
            "xn1T": np.ascontiguousarray(st["l1_xn1T"][:, sl]),
            "wg": f32c(ins["L1_ffn_w_gate"].T),
            "wfc": f32c(wfcT),
            "wpr": f32c(wprT),
            "x1": f32c(st["x1"][sl]),
        })
    return maps


def post_l1_out(st, res):
    st["x2"] = np.concatenate([r["x2"] for r in res], 0)



def build_l2_pre():
    pr = Prog("l2_pre")
    LT = 2176
    NTL = 17
    x_in = pr.din("x", [LT, D], F32)
    wq_in = pr.din("wqT", [D, D], F32)
    wk_in = pr.din("wkT", [D, D], F32)
    wv_in = pr.din("wvT", [D, D], F32)
    wb_in = pr.din("wbT", [D, 4], F32)
    cw_in = pr.din("cw3", [128, 3 * 8 * 4], F32)
    gw_in = pr.din("gwT", [D, 20], F32)
    gb_in = pr.din("gb", [1, 20], F32)
    lt_in = pr.din("ltemp", [1, 4], F32)
    ep_in = pr.din("epsp", [1, 4], F32)
    it_in = pr.din("it32", [1, 32], F32)
    tr_in = pr.din("trow", [1, 2560], F32)
    idf_in = pr.din("identf", [128, 128], F32)

    xnT_o = pr.dout("xn2T", [D, TPC], F32)
    qT_o = pr.dout("qT", [D, TPC], BF16)
    kT_o = pr.dout("kT", [D, TPC], BF16)
    vT_o = pr.dout("vT", [D, TPC], BF16)
    beta_o = pr.dout("beta", [TPC, 4], F32)
    probs_o = pr.dout("probs", [TPC, 20], F32)
    tab_o = pr.dout("tables", [4 * 64, 2560], F32)
    pr.constv(1e-5)

    xs = pr.sb("xs", [128, D], F32)
    xc = pr.sb("xc", [128, D], F32)
    sq = pr.sb("sq", [128, D], F32)
    mu = pr.sb("mu", [128, 1], F32)
    ss = pr.sb("ss", [128, 1], F32)
    rstd = pr.sb("rstd", [128, 1], F32)
    xnf = pr.sb("xnf", [128, D], F32)
    xf32 = pr.sb("xf32", [128, 128], F32)
    xn2T = pr.sb("xn2T", [128, 8 * LT], BF16)
    identf = pr.sb("identf", [128, 128], F32)
    wbuf = [pr.sb(f"wbuf{i}", [128, 8 * 128], BF16) for i in range(2)]
    qkv = pr.sb("qkv", [128, 8 * LT], BF16)
    cwl = pr.sb("cwl", [128, 3 * 8 * 4], F32)
    acc = pr.sb("acc", [128, TPC], F32)
    wbT = pr.sb("wbT", [128, 8 * 4], BF16)
    gwT = pr.sb("gwT", [128, 8 * 20], BF16)
    gbr = pr.sb("gbr", [1, 20], F32)
    ltr = pr.sb("ltr", [1, 4], F32)
    epr = pr.sb("epr", [1, 4], F32)
    erow = pr.sb("erow", [1, 20], F32)
    cb20 = pr.sb("cb20", [128, 3 * 20], F32)
    tv20 = pr.sb("tv20", [128, 20], F32)
    bt4 = pr.sb("bt4", [128, 4], F32)
    lg = pr.sb("lg", [128, 20], F32)
    lmx = pr.sb("lmx", [128, 4], F32)
    lsum = pr.sb("lsum", [128, 4], F32)
    pout = pr.sb("pout", [128, NTL * 20], F32)
    btt = pr.sb("btt", [128, NTL * 4], F32)
    ones1f = pr.sb("ones1f", [1, 128], F32)
    ivf = pr.sb("ivf", [1, 32], F32)
    bsl = pr.sb("bsl", [1, 32], F32)
    powr = pr.sb("powr", [1, 2560], F32)
    frq = pr.sb("frq", [32, 2560], F32)
    grq = pr.sb("grq", [32, 2560], F32)
    tb1 = pr.sb("tb1", [32, 2560], F32)
    tbi = pr.sb("tbi", [32, 2560], I32)
    tabs = pr.sb("tabs", [64, 4 * 2560], F32)

    pstf = pr.ps("pstf", [128, 128], F32)
    pm = [pr.ps(f"pm{i}", [128, 512], F32) for i in range(3)]
    p20 = pr.ps("p20", [128, 20], F32)
    p32 = pr.ps("p32", [32, 512], F32)

    def body(P):
        lid = P.sy.dma_start(out=identf[:, :], in_=idf_in[:, :])
        o1 = P.v.memset(ones1f[:, :], 1.0)
        lcw = P.sy.dma_start(out=cwl[:, :], in_=cw_in[:, :])
        lwb = P.g.dma_start(out=wbT.rearrange("p (c o) -> p c o", c=8),
                            in_=wb_in.rearrange("(c p) o -> p c o", p=128))
        lgw = P.g.dma_start(out=gwT.rearrange("p (c o) -> p c o", c=8),
                            in_=gw_in.rearrange("(c p) o -> p c o", p=128))
        lgb = P.sy.dma_start(out=gbr[:, :], in_=gb_in[:, :])
        llt = P.sy.dma_start(out=ltr[:, :], in_=lt_in[:, :])
        lep = P.sy.dma_start(out=epr[:, :], in_=ep_in[:, :])
        lit = P.sy.dma_start(out=ivf[:, :], in_=it_in[:, :])
        ltr2 = P.sy.dma_start(out=powr[:, :], in_=tr_in[:, :])

        t1 = P.s.activation(bsl[:, :], ivf[:, :], AF.Copy, scale=1.0 / 89.6, bias=25.6 / 89.6, dep=lit)
        t2 = P.s.activation(ivf[:, :], ivf[:, :], AF.Exp, scale=-math.log(10000.0) / 64.0, dep=[lit, t1])
        fr_toks = []
        for tc in range(5):
            m1 = P.t.matmul(p32[:, :], ivf[0:1, :], powr[0:1, tc * 512:(tc + 1) * 512],
                            start=True, stop=True, dep=[t2, ltr2] + fr_toks[-1:])
            fr_toks.append(P.v.tensor_copy(frq[:, tc * 512:(tc + 1) * 512], p32[:, :], dep=m1))
        TWO_PI = 2 * math.pi

        def sincos(phase, dep):
            p0 = P.v.tensor_scalar(grq[:, :], frq[:, :], phase, None, op0=ALU.add, dep=dep)
            p1 = P.v.tensor_scalar(tb1[:, :], grq[:, :], 1.0 / TWO_PI, None, op0=ALU.mult, dep=p0)
            p2 = P.v.tensor_copy(tbi[:, :], tb1[:, :], dep=p1)
            p3 = P.v.tensor_copy(tb1[:, :], tbi[:, :], dep=p2)
            p4 = P.v.scalar_tensor_tensor(grq[:, :], tb1[:, :], -TWO_PI, grq[:, :],
                                          op0=ALU.mult, op1=ALU.add, dep=p3)
            return P.s.activation(grq[:, :], grq[:, :], AF.Sin, dep=p4)

        sc1 = sincos(math.pi / 2, fr_toks)
        c1 = P.v.tensor_copy(tabs[0:32, 0 * 2560:1 * 2560], grq[:, :], dep=sc1)
        c2 = P.v.tensor_copy(tabs[32:64, 0 * 2560:1 * 2560], grq[:, :], dep=sc1)
        sc2 = sincos(0.0, [c1, c2] + fr_toks)
        c3 = P.v.tensor_copy(tabs[0:32, 1 * 2560:2 * 2560], grq[:, :], dep=sc2)
        c4 = P.v.tensor_copy(tabs[32:64, 1 * 2560:2 * 2560], grq[:, :], dep=sc2)
        l1 = P.s.activation(bsl[:, :], bsl[:, :], AF.Ln, dep=[t1, t2])
        p1 = P.v.tensor_scalar(powr[:, :], powr[:, :], -2048.0, 1.0 / 128.0, op0=ALU.add, op1=ALU.mult,
                               dep=[ltr2] + fr_toks)
        ps_toks = [c3, c4]
        for tc in range(5):
            m1 = P.t.matmul(p32[:, :], bsl[0:1, :], powr[0:1, tc * 512:(tc + 1) * 512],
                            start=True, stop=True, dep=[l1, p1] + ps_toks[-1:])
            ps_toks.append(P.v.tensor_copy(frq[:, tc * 512:(tc + 1) * 512], p32[:, :], dep=m1))
        e1 = P.s.activation(grq[:, :], frq[:, :], AF.Exp, dep=ps_toks)
        c5 = P.v.tensor_copy(tabs[0:32, 2 * 2560:3 * 2560], grq[:, :], dep=e1)
        c6 = P.v.tensor_copy(tabs[32:64, 2 * 2560:3 * 2560], grq[:, :], dep=e1)
        e2 = P.s.activation(grq[:, :], frq[:, :], AF.Exp, scale=-1.0, dep=[e1, c5, c6])
        c7 = P.v.tensor_copy(tabs[0:32, 3 * 2560:4 * 2560], grq[:, :], dep=e2)
        c8 = P.v.tensor_copy(tabs[32:64, 3 * 2560:4 * 2560], grq[:, :], dep=e2)
        st_tab = P.sy.dma_start(out=tab_o.rearrange("(c p) t -> p c t", p=64),
                                in_=tabs.rearrange("p (c t) -> p c t", c=4),
                                dep=[c1, c2, c3, c4, c5, c6, c7, c8])

        tr = None
        stf_toks = []
        for i in range(NTL):
            ld = P.sy.dma_start(out=xs[:, :], in_=x_in[i * 128:(i + 1) * 128, :], dep=tr)
            r = P.v.tensor_reduce(mu[:, :], xs[:, :], axis=AX.X, op=ALU.add, dep=ld)
            r = P.v.tensor_scalar(mu[:, :], mu[:, :], 1.0 / D, None, op0=ALU.mult, dep=r)
            r = P.v.tensor_scalar(xc[:, :], xs[:, :], mu[:, 0:1], None, op0=ALU.subtract, dep=r)
            r2 = P.s.activation(sq[:, :], xc[:, :], AF.Square, accum_out=ss[:, 0:1], dep=r)
            r3 = P.s.activation(rstd[:, :], ss[:, :], AF.Sqrt, bias=1e-5, scale=1.0 / D, dep=r2)
            r4 = P.v.reciprocal(rstd[:, :], rstd[:, :], dep=r3)
            r5 = P.v.tensor_scalar(xnf[:, :], xc[:, :], rstd[:, 0:1], None, op0=ALU.mult, dep=r4)
            lo_t, hi_t = i * 128, (i + 1) * 128
            lo_c, hi_c = max(lo_t, 3), min(hi_t, 3 + TPC)
            for f in range(8):
                tp = P.t.transpose(pstf[:, :], xnf[:, f * 128:(f + 1) * 128], identf[:, :],
                                   dep=[lid, r5] + ([tr] if tr else []) + stf_toks[-1:])
                cvf = P.v.tensor_copy(xf32[:, :], pstf[:, :], dep=tp)
                tr = P.v.tensor_copy(xn2T[:, f * LT + i * 128: f * LT + (i + 1) * 128], xf32[:, :], dep=cvf)
                if lo_c < hi_c:
                    stf_toks.append(P.sy.dma_start(
                        out=xnT_o[f * 128:(f + 1) * 128, lo_c - 3: hi_c - 3],
                        in_=xf32[:, lo_c - lo_t: hi_c - lo_t], dep=[cvf]))

        NCH = [512, 512, 512, 512, 128]
        outmap = [qT_o, kT_o, vT_o]
        wins = [wq_in, wk_in, wv_in]
        st_qkv = []
        wld = [None, None]
        prj_done = []
        for t3 in range(3):
            for mc in range(8):
                gmc = t3 * 8 + mc
                wb = wbuf[gmc % 2]
                lw = P.g.dma_start(out=wb.rearrange("p (c o) -> p c o", c=8),
                                   in_=wins[t3].rearrange("(c p) o -> p c o", p=128)[:, :, mc * 128:(mc + 1) * 128],
                                   dep=wld[gmc % 2])
                col = 0
                evs = []
                for ci, cn in enumerate(NCH):
                    pp = pm[ci % 2]
                    mt = None
                    for kc in range(8):
                        mt = P.t.matmul(pp[:, 0:cn], wb[:, kc * 128:(kc + 1) * 128],
                                        xn2T[:, kc * LT + col: kc * LT + col + cn],
                                        start=(kc == 0), stop=(kc == 7),
                                        dep=[lw, tr] + (evs[-2:-1] if len(evs) >= 2 else prj_done[-1:]))
                    evs.append(P.s.activation(qkv[:, mc * LT + col: mc * LT + col + cn],
                                              pp[:, 0:cn], AF.Copy, dep=[mt] + st_qkv[-8:]))
                    col += cn
                wld[gmc % 2] = evs[-1]
                prj_done.append(evs[-1])
                base = mc * LT
                a1 = P.v.tensor_scalar(acc[:, :], qkv[:, base + 0: base + TPC],
                                       cwl[:, gmc * 4 + 0: gmc * 4 + 1], None, op0=ALU.mult,
                                       dep=[evs[-1], lcw])
                for tap in range(1, 4):
                    a1 = P.v.scalar_tensor_tensor(acc[:, :], qkv[:, base + tap: base + tap + TPC],
                                                  cwl[:, gmc * 4 + tap: gmc * 4 + tap + 1], acc[:, :],
                                                  op0=ALU.mult, op1=ALU.add, dep=a1)
                a3 = P.s.activation(qkv[:, base + 3: base + 3 + TPC], acc[:, :], AF.Silu, dep=a1)
                prj_done.append(a3)
                st_qkv.append(P.sy.dma_start(out=outmap[t3][mc * 128:(mc + 1) * 128, :],
                                             in_=qkv[:, base + 3: base + 3 + TPC], dep=a3))

        tA = P.s.activation(ltr[:, :], ltr[:, :], AF.Exp, dep=llt)
        tB = P.v.reciprocal(ltr[:, :], ltr[:, :], dep=tA)
        ecl = P.v.tensor_scalar(epr[:, :], epr[:, :], 0.0, 0.2, op0=ALU.max, op1=ALU.min, dep=lep)
        rows = []
        for h4 in range(4):
            rows.append(P.v.tensor_copy(erow[:, h4 * 5:(h4 + 1) * 5],
                                        ltr[:, h4:h4 + 1].to_broadcast([1, 5]), dep=[tB] + rows[-1:]))
        bc1 = P.t.matmul(p20[:, :], ones1f[0:1, :], erow[0:1, :], start=True, stop=True, dep=rows)
        tvc = P.v.tensor_copy(tv20[:, :], p20[:, :], dep=bc1)
        rows2 = []
        for h4 in range(4):
            rows2.append(P.v.tensor_scalar(erow[:, h4 * 5:(h4 + 1) * 5],
                                           epr[:, h4:h4 + 1].to_broadcast([1, 5]), -5.0, 1.0,
                                           op0=ALU.mult, op1=ALU.add, dep=[ecl, tvc] + rows2[-1:]))
        bc2 = P.t.matmul(p20[:, :], ones1f[0:1, :], erow[0:1, :], start=True, stop=True, dep=rows2)
        e1c = P.v.tensor_copy(cb20[:, 20:40], p20[:, :], dep=bc2)
        rows3 = []
        for h4 in range(4):
            rows3.append(P.v.tensor_copy(erow[:, h4 * 5:(h4 + 1) * 5],
                                         epr[:, h4:h4 + 1].to_broadcast([1, 5]), dep=[e1c] + rows3[-1:]))
        bc3 = P.t.matmul(p20[:, :], ones1f[0:1, :], erow[0:1, :], start=True, stop=True, dep=rows3)
        e2c = P.v.tensor_copy(cb20[:, 40:60], p20[:, :], dep=bc3)
        bc4 = P.t.matmul(p20[:, :], ones1f[0:1, :], gbr[0:1, :], start=True, stop=True, dep=[lgb, e2c])
        gbc = P.v.tensor_copy(cb20[:, 0:20], p20[:, :], dep=bc4)

        bp_toks = []
        for i in range(16):
            coff = 3 + i * 128
            mt = None
            for kc in range(8):
                mt = P.t.matmul(p20[:, 0:4], xn2T[:, kc * LT + coff: kc * LT + coff + 128],
                                wbT[:, kc * 4:(kc + 1) * 4], start=(kc == 0), stop=(kc == 7),
                                dep=[lwb, tr, gbc] + bp_toks[-1:])
            b1 = P.s.activation(btt[:, i * 4:(i + 1) * 4], p20[:, 0:4], AF.Sigmoid, dep=mt)
            mt2 = None
            for kc in range(8):
                mt2 = P.t.matmul(p20[:, :], xn2T[:, kc * LT + coff: kc * LT + coff + 128],
                                 gwT[:, kc * 20:(kc + 1) * 20], start=(kc == 0), stop=(kc == 7),
                                 dep=[lgw, b1] + bp_toks[-1:])
            g1 = P.v.tensor_add(lg[:, :], p20[:, :], cb20[:, 0:20], dep=[mt2, gbc])
            g2 = P.v.tensor_mul(lg[:, :], lg[:, :], tv20[:, :], dep=[g1, tvc])
            g3 = P.v.tensor_reduce(lmx[:, :], lg.rearrange("p (h f) -> p h f", h=4), axis=AX.X,
                                   op=ALU.max, negate=True, dep=g2)
            g4 = P.v.tensor_tensor(lg.rearrange("p (h f) -> p h f", h=4),
                                   lg.rearrange("p (h f) -> p h f", h=4),
                                   lmx[:, :, None].to_broadcast([128, 4, 5]), ALU.add, dep=g3)
            g5 = P.s.activation(lg[:, :], lg[:, :], AF.Exp, dep=g4)
            g6 = P.v.tensor_reduce(lsum[:, :], lg.rearrange("p (h f) -> p h f", h=4), axis=AX.X,
                                   op=ALU.add, dep=g5)
            g7 = P.v.reciprocal(lsum[:, :], lsum[:, :], dep=g6)
            g8 = P.v.tensor_tensor(lg.rearrange("p (h f) -> p h f", h=4),
                                   lg.rearrange("p (h f) -> p h f", h=4),
                                   lsum[:, :, None].to_broadcast([128, 4, 5]), ALU.mult, dep=g7)
            g9 = P.v.tensor_mul(lg[:, :], lg[:, :], cb20[:, 20:40], dep=[g8, e1c])
            g10 = P.v.tensor_add(pout[:, i * 20:(i + 1) * 20], lg[:, :], cb20[:, 40:60], dep=[g9, e2c])
            bp_toks.append(g10)
        st_b = P.sy.dma_start(out=beta_o.rearrange("(n p) e -> p n e", p=128),
                              in_=btt[:, 0:64].rearrange("p (n e) -> p n e", e=4), dep=bp_toks)
        st_p = P.sy.dma_start(out=probs_o.rearrange("(n p) e -> p n e", p=128),
                              in_=pout[:, 0:320].rearrange("p (n e) -> p n e", e=20), dep=bp_toks)
        P.sy.nop(dep=[st_tab, st_b, st_p] + st_qkv[-1:] + stf_toks[-1:])

    return pr.build(body)


IDENTF32 = np.eye(128, dtype=np.float32)

def prep_l2_pre(st):
    ins = st["in"]
    x2 = st["x2"]
    cw3 = np.concatenate([
        ins["L2_attn_conv_q"].reshape(8, 128, 4).transpose(1, 0, 2).reshape(128, 32),
        ins["L2_attn_conv_k"].reshape(8, 128, 4).transpose(1, 0, 2).reshape(128, 32),
        ins["L2_attn_conv_v"].reshape(8, 128, 4).transpose(1, 0, 2).reshape(128, 32)], 1)
    maps = []
    for c in range(NCORES):
        l0 = (c % 2) * TPC
        xh = np.zeros((2176, D), np.float32)
        if c % 2 == 0:
            xh[3:3 + TPC] = x2[c * TPC:(c + 1) * TPC]
        else:
            xh[0:3 + TPC] = x2[c * TPC - 3:(c + 1) * TPC]
        trow = np.maximum(np.arange(l0 - 512, l0 + 2048, dtype=np.float32), 0.0).reshape(1, 2560)
        maps.append({
            "x": xh,
            "wqT": f32c(ins["L2_attn_Wq"].T),
            "wkT": f32c(ins["L2_attn_Wk"].T),
            "wvT": f32c(ins["L2_attn_Wv"].T),
            "wbT": f32c(ins["L2_attn_Wb"].T),
            "cw3": f32c(cw3),
            "gwT": f32c(ins["L2_attn_gate_w"].reshape(20, D).T),
            "gb": f32c(ins["L2_attn_gate_b"].reshape(1, 20)),
            "ltemp": f32c(ins["L2_attn_log_temp"].reshape(1, 4)),
            "epsp": f32c(ins["L2_attn_eps_param"].reshape(1, 4)),
            "it32": np.arange(0, 64, 2, dtype=np.float32).reshape(1, 32),
            "trow": trow,
            "identf": IDENTF32,
        })
    return maps


def post_l2_pre(st, res):
    st["l2_xn2T"] = np.concatenate([r["xn2T"] for r in res], 1)
    st["l2_qT"] = np.concatenate([r["qT"] for r in res], 1)
    st["l2_kT"] = np.concatenate([r["kT"] for r in res], 1)
    st["l2_vT"] = np.concatenate([r["vT"] for r in res], 1)
    st["l2_beta"] = np.concatenate([r["beta"] for r in res], 0)
    st["l2_probs"] = np.concatenate([r["probs"] for r in res], 0)
    st["l2_tabs"] = np.stack([r["tables"] for r in res], 0)


def build_l2_delta():
    pr = Prog("l2_delta")
    L = 4096
    NC_ = 32
    q_in = pr.din("qh", [2 * 256, L], BF16)
    k_in = pr.din("kh", [2 * 256, L], BF16)
    v_in = pr.din("vh", [2 * 256, L], BF16)
    b2_in = pr.din("bet2", [32, 2 * 128], F32)
    idb_in = pr.din("identb", [128, 128], BF16)
    idf_in = pr.din("identf32", [32, 32], F32)
    sl_in = pr.din("strictl32", [32, 32], F32)
    tu_in = pr.din("trius", [128, 128], F32)
    ti_in = pr.din("triui", [128, 128], F32)
    sel_in = pr.din("sel4", [128, 4 * 32], BF16)
    do_o = pr.dout("doT", [2 * 256, L], BF16)
    pr.constv(1e-6)

    qh = pr.sb("qh", [128, 4 * L], BF16)
    kh = pr.sb("kh", [128, 4 * L], BF16)
    vh = pr.sb("vh", [128, 4 * L], BF16)
    bet2 = pr.sb("bet2", [32, 2 * 128], F32)
    identb = pr.sb("identb", [128, 128], BF16)
    identf32 = pr.sb("identf32", [32, 32], F32)
    strictl32 = pr.sb("strictl32", [32, 32], F32)
    trius = pr.sb("trius", [128, 128], F32)
    triui = pr.sb("triui", [128, 128], F32)
    sel4 = pr.sb("sel4", [128, 4 * 32], BF16)
    onesc = pr.sb("onesc", [128, 1], BF16)
    onesr = pr.sb("onesr", [1, 128], BF16)
    sqb = pr.sb("sqb", [128, 2 * L], BF16)
    nrm = pr.sb("nrm", [1, L], F32)
    nrmb = pr.sb("nrmb", [1, L], BF16)
    ktok = pr.sb("ktok", [128, 256], BF16)
    vtok = pr.sb("vtok", [128, 256], BF16)
    symU = pr.sb("symU", [128, 128], BF16)
    Nd = pr.sb("Nd", [32, 32], F32)
    Xc = pr.sb("Xc", [32, 32], F32)
    XcT = pr.sb("XcT", [32, 32], F32)
    Pc = pr.sb("Pc", [32, 32], F32)
    PcT = pr.sb("PcT", [32, 32], F32)
    T32T = pr.sb("T32T", [32, 4 * 32], BF16)
    tblk = pr.sb("tblk", [32, 512], BF16)
    ublk = pr.sb("ublk", [32, 512], F32)
    uw = pr.sb("uw", [128, 512], BF16)
    atb = pr.sb("atb", [128, 128], BF16)
    uadj = pr.sb("uadj", [128, 256], BF16)
    wTb = pr.sb("wTb", [128, 2 * 128], BF16)
    Ssb = pr.sb("Ssb", [128, 2 * 256], BF16)
    Ssf = pr.sb("Ssf", [128, 2 * 256], F32)
    osb = pr.sb("osb", [128, 256], BF16)
    osb2 = pr.sb("osb2", [128, 256], BF16)

    pst = pr.ps("pst", [128, 128], BF16)
    pA = pr.ps("pA", [128, 128], F32)
    pB = pr.ps("pB", [128, 128], F32)
    pU = pr.ps("pU", [128, 512], F32)
    pO = pr.ps("pO", [128, 256], F32)
    pS = [pr.ps(f"pS{i}", [128, 256], F32) for i in range(2)]
    pd = pr.ps("pd", [32, 512], F32)

    def body(P):
        lq = P.sy.dma_start(out=qh.rearrange("p (c l) -> p c l", c=4),
                            in_=q_in.rearrange("(c p) l -> p c l", p=128))
        lk = P.sy.dma_start(out=kh.rearrange("p (c l) -> p c l", c=4),
                            in_=k_in.rearrange("(c p) l -> p c l", p=128))
        lv = P.sy.dma_start(out=vh.rearrange("p (c l) -> p c l", c=4),
                            in_=v_in.rearrange("(c p) l -> p c l", p=128))
        lb = P.sy.dma_start(out=bet2[:, :], in_=b2_in[:, :])
        li = P.sy.dma_start(out=identb[:, :], in_=idb_in[:, :])
        li2 = P.sy.dma_start(out=identf32[:, :], in_=idf_in[:, :])
        ls = P.sy.dma_start(out=strictl32[:, :], in_=sl_in[:, :])
        lu = P.sy.dma_start(out=trius[:, :], in_=tu_in[:, :])
        lt = P.sy.dma_start(out=triui[:, :], in_=ti_in[:, :])
        lsel = P.sy.dma_start(out=sel4[:, :], in_=sel_in[:, :])
        oc = P.v.memset(onesc[:, :], 1.0)
        oc2 = P.v.memset(onesr[:, :], 1.0, dep=oc)

        st_toks = []
        for hh in range(2):
            def l2n(src_sb, dep0):
                s1 = P.s.activation(sqb.rearrange("p (c l) -> p c l", c=2),
                                    src_sb.rearrange("p (c l) -> p c l", c=4)[:, 2 * hh:2 * hh + 2, :],
                                    AF.Square, dep=dep0)
                toks = []
                for tc in range(8):
                    mt = None
                    for dt_ in range(2):
                        mt = P.t.matmul(pd[0:1, :], onesc[:, 0:1],
                                        sqb[:, dt_ * L + tc * 512: dt_ * L + tc * 512 + 512],
                                        start=(dt_ == 0), stop=(dt_ == 1), dep=[s1, oc] + toks[-1:])
                    toks.append(P.v.tensor_copy(nrm[:, tc * 512:(tc + 1) * 512], pd[0:1, :], dep=mt))
                n1 = P.s.activation(nrm[:, :], nrm[:, :], AF.Ln, bias=1e-6, dep=toks)
                n2 = P.s.activation(nrm[:, :], nrm[:, :], AF.Exp, scale=-0.5, dep=n1)
                n3 = P.v.tensor_copy(nrmb[:, :], nrm[:, :], dep=n2)
                outks = [n3]
                for tc in range(8):
                    bm = P.t.matmul(pU[:, :], onesr[0:1, :], nrmb[0:1, tc * 512:(tc + 1) * 512],
                                    start=True, stop=True, dep=[n3, oc2] + outks[-1:])
                    for dt_ in range(2):
                        outks.append(P.v.tensor_tensor(
                            src_sb.rearrange("p (c l) -> p c l", c=4)[:, 2 * hh + dt_, tc * 512:(tc + 1) * 512],
                            src_sb.rearrange("p (c l) -> p c l", c=4)[:, 2 * hh + dt_, tc * 512:(tc + 1) * 512],
                            pU[:, :], ALU.mult, dep=[bm] + outks[-1:]))
                return outks
            qn_t = l2n(qh, [lq] + st_toks[-1:])
            kn_t = l2n(kh, [lk] + qn_t[-1:])

            z0 = P.v.memset(Ssf[:, :], 0.0, dep=st_toks[-1:])
            z1 = P.v.memset(Ssb[:, :], 0.0, dep=z0)
            sdone = [z1]
            for ch in range(NC_):
                cl = slice(ch * 128, (ch + 1) * 128)
                tks = []
                for dt_ in range(2):
                    tp = P.t.transpose(pst[:, :], kh[:, (2 * hh + dt_) * L + ch * 128:(2 * hh + dt_) * L + (ch + 1) * 128],
                                       identb[:, :], dep=kn_t + [li] + sdone[-1:] + tks[-1:])
                    tks.append(P.v.tensor_copy(ktok[:, dt_ * 128:(dt_ + 1) * 128], pst[:, :], dep=tp))
                    tp2 = P.t.transpose(pst[:, :], vh[:, (2 * hh + dt_) * L + ch * 128:(2 * hh + dt_) * L + (ch + 1) * 128],
                                        identb[:, :], dep=[lv, li, tks[-1]])
                    tks.append(P.v.tensor_copy(vtok[:, dt_ * 128:(dt_ + 1) * 128], pst[:, :], dep=tp2))
                mt = None
                for dt_ in range(2):
                    mt = P.t.matmul(pA[:, :], kh[:, (2 * hh + dt_) * L + ch * 128:(2 * hh + dt_) * L + (ch + 1) * 128],
                                    kh[:, (2 * hh + dt_) * L + ch * 128:(2 * hh + dt_) * L + (ch + 1) * 128],
                                    start=(dt_ == 0), stop=(dt_ == 1), dep=kn_t + tks)
                sU = P.v.tensor_mul(symU[:, :], pA[:, :], trius[:, :], dep=[mt, lu])
                mt2 = None
                for dt_ in range(2):
                    mt2 = P.t.matmul(pB[:, :], kh[:, (2 * hh + dt_) * L + ch * 128:(2 * hh + dt_) * L + (ch + 1) * 128],
                                     qh[:, (2 * hh + dt_) * L + ch * 128:(2 * hh + dt_) * L + (ch + 1) * 128],
                                     start=(dt_ == 0), stop=(dt_ == 1), dep=qn_t + [sU])
                at1 = P.v.tensor_mul(atb[:, :], pB[:, :], triui[:, :], dep=[mt2, lt])

                ap_toks = []
                for i4 in range(4):
                    bcol = bet2[:, hh * 128 + ch * 4 + i4: hh * 128 + ch * 4 + i4 + 1]
                    md = None
                    for dt_ in range(2):
                        md = P.t.matmul(pd[:, 0:32],
                                        kh[:, (2 * hh + dt_) * L + ch * 128 + i4 * 32:(2 * hh + dt_) * L + ch * 128 + i4 * 32 + 32],
                                        kh[:, (2 * hh + dt_) * L + ch * 128 + i4 * 32:(2 * hh + dt_) * L + ch * 128 + i4 * 32 + 32],
                                        start=(dt_ == 0), stop=(dt_ == 1), dep=kn_t + [at1] + ap_toks[-1:])
                    nd1 = P.v.tensor_scalar(Nd[:, :], pd[:, 0:32], bcol, None, op0=ALU.mult, dep=[md, lb])
                    nd2 = P.v.tensor_scalar(Nd[:, :], Nd[:, :], -1.0, None, op0=ALU.mult, dep=nd1)
                    nd3 = P.v.tensor_mul(Nd[:, :], Nd[:, :], strictl32[:, :], dep=[nd2, ls])
                    tpd = P.t.transpose(pd[:, 0:32], Nd[:, :], identf32[:, :], dep=[nd3, li2])
                    cnT = P.v.tensor_copy(XcT[:, :], pd[:, 0:32], dep=tpd)
                    cn1 = P.v.tensor_copy(Xc[:, :], Nd[:, :], dep=cnT)
                    pc = P.v.tensor_add(Pc[:, :], Nd[:, :], identf32[:, :], dep=cn1)
                    ptc = P.v.tensor_add(PcT[:, :], XcT[:, :], identf32[:, :], dep=pc)
                    for lvl in range(4):
                        m1 = P.t.matmul(pd[:, 0:32], XcT[:, :], Xc[:, :], start=True, stop=True,
                                        dep=[pc, ptc])
                        cx1 = P.v.tensor_copy(Nd[:, :], pd[:, 0:32], dep=m1)
                        m2 = P.t.matmul(pd[:, 0:32], Xc[:, :], XcT[:, :], start=True, stop=True, dep=cx1)
                        cx2 = P.v.tensor_copy(XcT[:, :], pd[:, 0:32], dep=m2)
                        cx3 = P.v.tensor_copy(Xc[:, :], Nd[:, :], dep=cx2)
                        m3 = P.t.matmul(pd[:, 0:32], PcT[:, :], Xc[:, :], start=True, stop=True, dep=cx3)
                        pc = P.v.tensor_add(Pc[:, :], Pc[:, :], pd[:, 0:32], dep=m3)
                        m4 = P.t.matmul(pd[:, 0:32], Xc[:, :], PcT[:, :], start=True, stop=True, dep=pc)
                        ptc = P.v.tensor_add(PcT[:, :], PcT[:, :], pd[:, 0:32], dep=m4)
                    ctt = P.v.tensor_copy(T32T[:, i4 * 32:(i4 + 1) * 32], PcT[:, :], dep=[pc, ptc])
                    nmm = 2 + i4
                    mj = P.t.matmul(pd[:, 0:256], sel4[:, i4 * 32:(i4 + 1) * 32], vtok[:, :],
                                    start=True, stop=(nmm == 1), dep=tks + [ctt, lsel] + ap_toks[-1:])
                    mj = P.t.matmul(pd[:, 256:512], sel4[:, i4 * 32:(i4 + 1) * 32], ktok[:, :],
                                    start=False, stop=(nmm == 2), dep=[mj])
                    for j4 in range(i4):
                        mj = P.t.matmul(pd[:, :], symU[32 * j4:32 * j4 + 32, i4 * 32:(i4 + 1) * 32],
                                        uw[32 * j4:32 * j4 + 32, :],
                                        start=False, stop=(j4 == i4 - 1), dep=[mj, sU] + ap_toks)
                    t1 = P.v.tensor_scalar(tblk[:, :], pd[:, :], bcol, None, op0=ALU.mult, dep=[mj])
                    ma = P.t.matmul(pd[:, :], T32T[:, i4 * 32:(i4 + 1) * 32], tblk[:, :],
                                    start=True, stop=True, dep=[t1, ctt])
                    cu = P.v.tensor_copy(ublk[:, :], pd[:, :], dep=ma)
                    if i4 < 3:
                        ap_toks.append(P.v.tensor_copy(uw[32 * i4:32 * i4 + 32, :], ublk[:, :], dep=cu))
                    else:
                        ap_toks.append(P.g.dma_start(out=uw[96:128, :], in_=ublk[:, :], dep=cu))
                cwts = []
                for dt_ in range(2):
                    tpw = P.t.transpose(pst[:, :], uw[:, 256 + dt_ * 128: 256 + (dt_ + 1) * 128],
                                        identb[:, :], dep=ap_toks + cwts[-1:])
                    cwts.append(P.v.tensor_copy(wTb[:, dt_ * 128:(dt_ + 1) * 128], pst[:, :], dep=tpw))
                mt3 = None
                for dt_ in range(2):
                    mt3 = P.t.matmul(pU[:, 0:256], wTb[:, dt_ * 128:(dt_ + 1) * 128],
                                     Ssb[:, dt_ * 256:(dt_ + 1) * 256],
                                     start=(dt_ == 0), stop=(dt_ == 1),
                                     dep=cwts + [sdone[-1]] + ([mt3] if mt3 else []))
                ua = P.v.tensor_tensor(uadj[:, :], uw[:, 0:256], pU[:, 0:256], ALU.subtract,
                                       dep=[mt3] + ap_toks)
                mo = None
                for dt_ in range(2):
                    mo = P.t.matmul(pO[:, :], qh[:, (2 * hh + dt_) * L + ch * 128:(2 * hh + dt_) * L + (ch + 1) * 128],
                                    Ssb[:, dt_ * 256:(dt_ + 1) * 256],
                                    start=(dt_ == 0), stop=False, dep=[ua, sdone[-1]] + qn_t)
                mo = P.t.matmul(pO[:, :], atb[:, :], uadj[:, :], start=False, stop=True, dep=[mo, ua, at1])
                co = P.v.tensor_copy(osb[:, :], pO[:, :], dep=mo)
                for dt_ in range(2):
                    tp4 = P.t.transpose(pst[:, :], osb[:, dt_ * 128:(dt_ + 1) * 128], identb[:, :],
                                        dep=[co, li] + st_toks[-2:])
                    cv4 = P.v.tensor_copy(osb2[:, dt_ * 128:(dt_ + 1) * 128], pst[:, :], dep=tp4)
                    st_toks.append(P.sy.dma_start(
                        out=do_o[(2 * hh + dt_) * 128:(2 * hh + dt_ + 1) * 128, cl],
                        in_=osb2[:, dt_ * 128:(dt_ + 1) * 128], dep=cv4))
                snew = []
                for dt_ in range(2):
                    ms = P.t.matmul(pS[dt_][:, :], ktok[:, dt_ * 128:(dt_ + 1) * 128], uadj[:, :],
                                    start=True, stop=True, dep=[ua] + tks + st_toks[-2:] + sdone)
                    s1_ = P.v.tensor_add(Ssf[:, dt_ * 256:(dt_ + 1) * 256],
                                         Ssf[:, dt_ * 256:(dt_ + 1) * 256], pS[dt_][:, :],
                                         dep=[ms] + sdone)
                    snew.append(P.v.tensor_copy(Ssb[:, dt_ * 256:(dt_ + 1) * 256],
                                                Ssf[:, dt_ * 256:(dt_ + 1) * 256], dep=s1_))
                sdone = snew
        P.sy.nop(dep=st_toks[-1])

    return pr.build(body)


STRICTL128 = (-np.tril(np.ones((128, 128), dtype=np.float32), -1))
STRICTL32P = np.tril(np.ones((32, 32), dtype=np.float32), -1)
TRIUS128 = -np.triu(np.ones((128, 128), dtype=np.float32), 1)
TRIUI128 = np.triu(np.ones((128, 128), dtype=np.float32), 0)
IDENTF32_32 = np.eye(32, dtype=np.float32)
SEL4 = np.zeros((128, 128), dtype=NBF)
for _i in range(4):
    SEL4[_i * 32:(_i + 1) * 32, _i * 32:(_i + 1) * 32] = np.eye(32, dtype=NBF)

def prep_l2_delta(st):
    maps = []
    for c in range(NCORES):
        b = c // 2
        h0 = 2 * (c % 2)
        cols = slice(b * 4096, (b + 1) * 4096)
        qs = np.concatenate([st["l2_qT"][(h0 + i) * 256:(h0 + i + 1) * 256, cols] for i in range(2)], 0)
        ks = np.concatenate([st["l2_kT"][(h0 + i) * 256:(h0 + i + 1) * 256, cols] for i in range(2)], 0)
        vs = np.concatenate([st["l2_vT"][(h0 + i) * 256:(h0 + i + 1) * 256, cols] for i in range(2)], 0)
        bet = st["l2_beta"][cols.start:cols.stop, h0:h0 + 2]
        b4 = bet.T.reshape(2, 32, 4, 32).transpose(3, 0, 1, 2).reshape(32, 256)
        maps.append({
            "qh": np.ascontiguousarray(qs),
            "kh": np.ascontiguousarray(ks),
            "vh": np.ascontiguousarray(vs),
            "bet2": f32c(b4),
            "identb": IDENT128,
            "identf32": IDENTF32_32,
            "strictl32": STRICTL32P,
            "trius": TRIUS128,
            "triui": TRIUI128,
            "sel4": SEL4,
        })
    return maps


def post_l2_delta(st, res):
    doT = np.zeros((1024, N_TOK), NBF)
    for c in range(NCORES):
        b = c // 2
        h0 = 2 * (c % 2)
        cols = slice(b * 4096, (b + 1) * 4096)
        for i in range(2):
            doT[(h0 + i) * 256:(h0 + i + 1) * 256, cols] = res[c]["doT"][i * 256:(i + 1) * 256]
    st["l2_doT"] = doT



def build_l2_mix():
    pr = Prog("l2_mix")
    LH = 2112
    v_in = pr.din("vh", [D, LH], BF16)
    do_in = pr.din("doT", [D, TPC], BF16)
    pb_in = pr.din("probsT", [20, TPC], F32)
    fir_in = pr.din("fir3", [128, 8 * 81], F32)
    nw_in = pr.din("nw8", [128, 8], F32)
    wo_in = pr.din("woT", [D, D], F32)
    idf_in = pr.din("identf", [128, 128], F32)
    at_o = pr.dout("attn2T", [D, TPC], BF16)
    pr.constv(1e-5)

    vh = pr.sb("vh", [128, 8 * LH], BF16)
    dot = pr.sb("dot", [128, TPC], BF16)
    fir = pr.sb("fir", [128, 8 * 81], F32)
    nw8 = pr.sb("nw8", [128, 8], F32)
    identf = pr.sb("identf", [128, 128], F32)
    prow1 = pr.sb("prow1", [1, TPC], BF16)
    oac = [pr.sb(f"oac{i}", [128, TPC], F32) for i in range(2)]
    tmp = pr.sb("tmp", [128, TPC], F32)
    tm2 = pr.sb("tm2", [128, 512], F32)
    sqf = pr.sb("sqf", [128, TPC], BF16)
    nrow = pr.sb("nrow", [1, TPC], F32)
    oTb = pr.sb("oTb", [128, 8 * TPC], BF16)
    aout = pr.sb("aout", [128, TPC], BF16)
    woT = pr.sb("woT", [128, 8 * D], BF16)
    onesc = pr.sb("onesc", [128, 1], BF16)
    onesrw = pr.sb("onesrw", [1, 128], BF16)
    nrowb = pr.sb("nrowb", [1, TPC], BF16)
    pbc = pr.ps("pbc", [128, 512], F32)
    pn4 = pr.ps("pn4", [1, 2048], F32)
    pm = [pr.ps(f"pm{i}", [128, 512], F32) for i in range(2)]

    def body(P):
        lv = P.sy.dma_start(out=vh.rearrange("p (c l) -> p c l", c=8),
                            in_=v_in.rearrange("(c p) l -> p c l", p=128))
        lf = P.sy.dma_start(out=fir[:, :], in_=fir_in[:, :])
        lnw = P.sy.dma_start(out=nw8[:, :], in_=nw_in[:, :])
        lid = P.sy.dma_start(out=identf[:, :], in_=idf_in[:, :])
        lwo = P.g.dma_start(out=woT.rearrange("p (c o) -> p c o", c=8),
                            in_=wo_in.rearrange("(c p) o -> p c o", p=128))
        oc = P.v.memset(onesc[:, :], 1.0)
        ocr = P.v.memset(onesrw[:, :], 1.0, dep=oc)

        pts = []

        KOF = {0: (3, 62, 0), 1: (15, 50, 3), 2: (63, 2, 18)}
        obt = []
        for h in range(4):
            octoks = [[], []]
            ld_toks = []
            for sub in range(2):
                ct = 2 * h + sub
                base = ct * LH
                oa = oac[sub]
                ldd = P.g.dma_start(out=dot[:, :], in_=do_in[ct * 128:(ct + 1) * 128, :],
                                    dep=(octoks[0][-1:] if sub == 1 else (obt[-1:] if obt else [])))
                ld_toks.append(ldd)
                for path in range(5):
                    if path < 3:
                        K, off, fb = KOF[path]
                        fbase = ct * 81 + fb
                        a1 = P.v.tensor_scalar(tmp[:, :], vh[:, base + off: base + off + TPC],
                                               fir[:, fbase:fbase + 1], None, op0=ALU.mult,
                                               dep=[lv, lf] + (octoks[sub][-1:] if octoks[sub] else obt[-1:]))
                        for tap in range(1, K):
                            a1 = P.v.scalar_tensor_tensor(tmp[:, :], vh[:, base + off + tap: base + off + tap + TPC],
                                                          fir[:, fbase + tap: fbase + tap + 1], tmp[:, :],
                                                          op0=ALU.mult, op1=ALU.add, dep=a1)
                        sdep = [a1]
                    elif path == 3:
                        sdep = [ldd]
                    else:
                        sdep = [lv]
                    lrow = P.g.dma_start(out=prow1[:, :], in_=pb_in[h * 5 + path: h * 5 + path + 1, :],
                                          dep=pts[-1:])
                    for tc in range(4):
                        if path < 3:
                            sap = tmp[:, tc * 512:(tc + 1) * 512]
                        elif path == 3:
                            sap = dot[:, tc * 512:(tc + 1) * 512]
                        else:
                            sap = vh[:, base + 64 + tc * 512: base + 64 + tc * 512 + 512]
                        bm = P.t.matmul(pbc[:, :], onesrw[0:1, :],
                                        prow1[0:1, tc * 512:(tc + 1) * 512],
                                        start=True, stop=True,
                                        dep=[lrow] + sdep + [ocr] + (octoks[sub][-1:] if octoks[sub] else []))
                        pts.append(bm)
                        if path == 0:
                            octoks[sub].append(P.v.tensor_tensor(
                                oa[:, tc * 512:(tc + 1) * 512], sap, pbc[:, :], ALU.mult,
                                dep=[bm] + sdep))
                        else:
                            m1 = P.v.tensor_tensor(tm2[:, :], sap, pbc[:, :], ALU.mult,
                                                   dep=[bm] + sdep + octoks[sub][-1:])
                            octoks[sub].append(P.v.tensor_add(
                                oa[:, tc * 512:(tc + 1) * 512], oa[:, tc * 512:(tc + 1) * 512],
                                tm2[:, :], dep=m1))
            sq_t = P.s.activation(sqf[:, :], oac[0][:, :], AF.Square, dep=octoks[0] + octoks[1])
            mms = []
            for tc in range(4):
                mms.append(P.t.matmul(pn4[0:1, tc * 512:(tc + 1) * 512], onesc[:, 0:1],
                                      sqf[:, tc * 512:(tc + 1) * 512], start=True, stop=False,
                                      dep=[sq_t, oc] + (obt[-1:] if obt else [])))
            sq_t2 = P.s.activation(sqf[:, :], oac[1][:, :], AF.Square, dep=[sq_t] + mms)
            cn = []
            for tc in range(4):
                m2 = P.t.matmul(pn4[0:1, tc * 512:(tc + 1) * 512], onesc[:, 0:1],
                                sqf[:, tc * 512:(tc + 1) * 512], start=False, stop=True,
                                dep=[sq_t2] + mms)
                cn.append(P.v.tensor_copy(nrow[:, tc * 512:(tc + 1) * 512],
                                          pn4[0:1, tc * 512:(tc + 1) * 512], dep=m2))
            n1 = P.s.activation(nrow[:, :], nrow[:, :], AF.Ln, bias=1e-5, scale=1.0 / 256.0, dep=cn)
            n2 = P.s.activation(nrow[:, :], nrow[:, :], AF.Exp, scale=-0.5, dep=n1)
            n3x = P.v.tensor_copy(nrowb[:, :], nrow[:, :], dep=n2)
            for sub in range(2):
                ct = 2 * h + sub
                for tc in range(4):
                    bm = P.t.matmul(pbc[:, :], onesrw[0:1, :], nrowb[0:1, tc * 512:(tc + 1) * 512],
                                    start=True, stop=True, dep=[n3x, ocr] + obt[-1:])
                    mx = P.v.tensor_tensor(tm2[:, :], oac[sub][:, tc * 512:(tc + 1) * 512],
                                           pbc[:, :], ALU.mult, dep=[bm] + obt[-1:])
                    obt.append(P.v.tensor_scalar(oTb[:, ct * TPC + tc * 512: ct * TPC + tc * 512 + 512],
                                                 tm2[:, :], nw8[:, ct:ct + 1], None, op0=ALU.mult,
                                                 dep=[mx, lnw]))

        st_toks = []
        for mc in range(8):
            evs = []
            for tc in range(4):
                pp = pm[tc % 2]
                mt = None
                for kc in range(8):
                    mt = P.t.matmul(pp[:, :], woT[:, kc * D + mc * 128: kc * D + mc * 128 + 128],
                                    oTb[:, kc * TPC + tc * 512: kc * TPC + tc * 512 + 512],
                                    start=(kc == 0), stop=(kc == 7),
                                    dep=[lwo] + obt + (evs[-2:-1] if len(evs) >= 2 else st_toks[-1:]))
                evs.append(P.s.activation(aout[:, tc * 512:(tc + 1) * 512], pp[:, :], AF.Copy, dep=mt))
            st_toks.append(P.sy.dma_start(out=at_o[mc * 128:(mc + 1) * 128, :], in_=aout[:, :], dep=evs))
        P.sy.nop(dep=st_toks[-1])

    return pr.build(body)


def prep_l2_mix(st):
    ins = st["in"]
    f3 = np.concatenate([ins["L2_attn_fir_s"].reshape(1024, 3),
                         ins["L2_attn_fir_m"].reshape(1024, 15),
                         ins["L2_attn_fir_l"].reshape(1024, 63)], 1)
    fir3 = f3.reshape(8, 128, 81).transpose(1, 0, 2).reshape(128, 8 * 81)
    nw8 = np.tile(ins["L2_attn_norm_w"], 4).reshape(8, 128).T
    vfull = st["l2_vT"]
    maps = []
    for c in range(NCORES):
        sl = slice(c * TPC, (c + 1) * TPC)
        vhal = np.zeros((D, LH2MIX), NBF)
        lo = c * TPC - 64
        if c % 2 == 0:
            vhal[:, 64:] = vfull[:, sl]
        else:
            vhal[:, :] = vfull[:, lo:(c + 1) * TPC]
        maps.append({
            "vh": np.ascontiguousarray(vhal),
            "doT": np.ascontiguousarray(st["l2_doT"][:, sl]),
            "probsT": f32c(st["l2_probs"][sl].T),
            "fir3": f32c(fir3),
            "nw8": f32c(nw8),
            "woT": f32c(ins["L2_attn_Wo"].T),
            "identf": IDENTF32,
        })
    return maps

LH2MIX = 2112

def post_l2_mix(st, res):
    st["l2_attnT"] = np.concatenate([r["attn2T"] for r in res], 1)


LHF = 2560
ATW = 2304

def build_l2fa():
    pr = Prog("l2fa")
    ys_in = pr.din("yfs", [D, LHF], F32)
    wq_in = pr.din("wqkv", [D, 3072], F32)
    tab_in = pr.din("tabs", [4 * 64, LHF], F32)
    qs_in = pr.din("qscol", [64, 2], F32)
    ks_in = pr.din("kscol", [64, 2], F32)
    rot_in = pr.din("rotm", [128, 128], BF16)
    hm_in = pr.din("hm8", [128, 8 * 16], BF16)
    rp_in = pr.din("repl16", [16, 8 * 128], BF16)
    qT_o = pr.dout("qfT", [D, LHF], BF16)
    kT_o = pr.dout("kfT", [D, LHF], BF16)
    v_o = pr.dout("vtok", [LHF, D], BF16)
    qwm_o = pr.dout("qwm", [16, 10], F32)
    kwm_o = pr.dout("kwm", [16, 10], F32)
    pr.constv(1e-12)

    ysb = pr.sb("ysb", [128, 8 * LHF], BF16)
    qT = pr.sb("qT", [128, 8 * LHF], BF16)
    kT = pr.sb("kT", [128, 8 * LHF], BF16)
    wbuf = [pr.sb(f"wbuf{i}", [128, 8 * 128], BF16) for i in range(2)]
    vbuf = [pr.sb("vbuf0", [128, 8 * 512], BF16)] * 2
    vstage = pr.sb("vstage", [128, D], BF16)
    ta = pr.sb("ta", [64, LHF], F32)
    tbl = pr.sb("tbl", [64, LHF], F32)
    cs4 = pr.sb("cs4", [128, 4 * LHF], BF16)
    qsc = pr.sb("qsc", [64, 2], F32)
    ksc = pr.sb("ksc", [64, 2], F32)
    stg = pr.sb("stg", [64, LHF], F32)
    rotm = pr.sb("rotm", [128, 128], BF16)
    hm8 = pr.sb("hm8", [128, 8 * 16], BF16)
    repl = pr.sb("repl", [16, 8 * 128], BF16)
    sqc = pr.sb("sqc", [128, 512], BF16)
    qn2 = pr.sb("qn2", [16, LHF], F32)
    rqb = pr.sb("rqb", [16, LHF], BF16)
    qnb = pr.sb("qnb", [128, 512], BF16)
    t1b = pr.sb("t1b", [128, 512], BF16)
    t2b = pr.sb("t2b", [128, 512], BF16)
    qsc2 = pr.sb("qsc2", [128, 1], F32)
    ksc2 = pr.sb("ksc2", [128, 1], F32)
    sqb = pr.sb("sqb", [128, 512], BF16)
    wmx = pr.sb("wmx", [16, 10], F32)

    pm = [pr.ps(f"pm{i}", [128, 512], F32) for i in range(2)]
    p16n = pr.ps("p16n", [16, 512], F32)
    prot = pr.ps("prot", [128, 512], F32)

    nsum = stg[0:16, :]

    def body(P):
        lys = P.g.dma_start(out=ysb.rearrange("p (c l) -> p c l", c=8),
                            in_=ys_in.rearrange("(c p) l -> p c l", p=128))
        lqs = P.sy.dma_start(out=qsc[:, :], in_=qs_in[:, :])
        lks = P.sy.dma_start(out=ksc[:, :], in_=ks_in[:, :])
        lrot = P.sy.dma_start(out=rotm[:, :], in_=rot_in[:, :])
        lhm = P.sy.dma_start(out=hm8[:, :], in_=hm_in[:, :])
        lrp = P.sy.dma_start(out=repl[:, :], in_=rp_in[:, :])
        lq2a = P.sy.dma_start(out=qsc2[0:64, 0:1], in_=qs_in[0:64, 0:1], allow_slow_non_contiguous=True)
        lq2b = P.sy.dma_start(out=qsc2[64:128, 0:1], in_=qs_in[0:64, 0:1], allow_slow_non_contiguous=True)
        lk2a = P.sy.dma_start(out=ksc2[0:64, 0:1], in_=ks_in[0:64, 0:1], allow_slow_non_contiguous=True)
        lk2b = P.sy.dma_start(out=ksc2[64:128, 0:1], in_=ks_in[0:64, 0:1], allow_slow_non_contiguous=True)
        lsc2 = [lq2a, lq2b, lk2a, lk2b]

        tt_toks = []
        specs = [(0, 0, qsc, 0, 128), (1, 64, qsc, 1, 128),
                 (2, 0, ksc, 0, 192), (3, 64, ksc, 1, 192)]
        for (idx, trig_off, scol, sidx, inv_row0) in specs:
            lta = P.sy.dma_start(out=ta[:, :], in_=tab_in[trig_off:trig_off + 64, :], dep=tt_toks[-1:])
            ltb = P.sy.dma_start(out=tbl[:, :], in_=tab_in[inv_row0:inv_row0 + 64, :], dep=tt_toks[-1:])
            m1 = P.v.tensor_mul(stg[:, :], ta[:, :], tbl[:, :],
                                dep=[lta, ltb] + tt_toks[-1:])
            m2 = P.v.tensor_scalar(stg[:, :], stg[:, :], scol[:, sidx:sidx + 1],
                                   (8.0 if idx < 2 else None), op0=ALU.mult,
                                   op1=(ALU.mult if idx < 2 else ALU.bypass),
                                   dep=[m1, lqs, lks])
            c1 = P.v.tensor_copy(cs4[0:64, idx * LHF:(idx + 1) * LHF], stg[:, :], dep=m2)
            tt_toks.append(P.v.tensor_copy(cs4[64:128, idx * LHF:(idx + 1) * LHF], stg[:, :], dep=c1))

        vst = []
        for nc2 in range(2):
            vld = P.g.dma_start(out=vbuf[0].rearrange("p (c o) -> p c o", c=8),
                                in_=wq_in.rearrange("(c p) o -> p c o", p=128)[:, :, 2048 + nc2 * 512: 2048 + (nc2 + 1) * 512],
                                dep=vst[-1:])
            for it in range(20):
                pp = pm[it % 2]
                mt = None
                for kc in range(8):
                    mt = P.t.matmul(pp[:, :], ysb[:, kc * LHF + it * 128: kc * LHF + (it + 1) * 128],
                                    vbuf[0][:, kc * 512:(kc + 1) * 512],
                                    start=(kc == 0), stop=(kc == 7),
                                    dep=[vld, lys] + (vst[-2:-1] if len(vst) >= 2 else []))
                ev = P.s.activation(vstage[:, 0:512], pp[:, :], AF.Copy, dep=[mt] + vst[-1:])
                vst.append(P.sy.dma_start(out=v_o[it * 128:(it + 1) * 128, nc2 * 512:(nc2 + 1) * 512],
                                          in_=vstage[:, 0:512], dep=ev))

        def proj(dst, col0, dep0):
            toks = []
            wld2 = [None, None]
            for mc in range(8):
                wb = wbuf[mc % 2]
                lw = P.g.dma_start(out=wb.rearrange("p (c o) -> p c o", c=8),
                                   in_=wq_in.rearrange("(c p) o -> p c o", p=128)[:, :, col0 + mc * 128: col0 + (mc + 1) * 128],
                                   dep=[wld2[mc % 2] if wld2[mc % 2] else dep0])
                for ci in range(5):
                    pp = pm[ci % 2]
                    mt = None
                    for kc in range(8):
                        mt = P.t.matmul(pp[:, :], wb[:, kc * 128:(kc + 1) * 128],
                                        ysb[:, kc * LHF + ci * 512: kc * LHF + ci * 512 + 512],
                                        start=(kc == 0), stop=(kc == 7),
                                        dep=[lw, lys] + (toks[-2:] if len(toks) >= 2 else [dep0]))
                    toks.append(P.s.activation(dst[:, mc * LHF + ci * 512: mc * LHF + ci * 512 + 512],
                                               pp[:, :], AF.Copy, dep=mt))
                wld2[mc % 2] = toks[-1]
            return toks
        qtk = proj(qT, 0, vst[-1])
        ktk = proj(kT, 1024, qtk[-1])

        def normrot(src, tabq, ndep, scol2, wm_out, qfac):
            cps = []
            for tc in range(5):
                mml = None
                for c in range(8):
                    s1 = P.s.activation(sqc[:, :], src[:, c * LHF + tc * 512: c * LHF + tc * 512 + 512],
                                        AF.Square, dep=ndep + cps[-1:] + ([mml] if mml else []))
                    mml = P.t.matmul(p16n[:, :], hm8[:, c * 16:(c + 1) * 16], sqc[:, :],
                                     start=(c == 0), stop=(c == 7), dep=[s1, lhm])
                cps.append(P.v.tensor_copy(qn2[:, tc * 512:(tc + 1) * 512],
                                           p16n[:, :], dep=mml))
            r1 = P.s.activation(qn2[:, :], qn2[:, :], AF.Sqrt, dep=cps)
            r2 = P.v.tensor_scalar(qn2[:, :], qn2[:, :], 1e-12, None, op0=ALU.max, dep=r1)
            r3 = P.v.reciprocal(qn2[:, :], qn2[:, :], dep=r2)
            r4 = P.v.tensor_copy(rqb[:, :], qn2[:, :], dep=r3)
            nz = P.v.memset(nsum[:, :], 0.0, dep=ndep + tt_toks)
            out_toks = [r4]
            nadd = nz
            for c in range(8):
                for tc in range(5):
                    b1 = P.t.matmul(pm[0][:, :], repl[:, c * 128:(c + 1) * 128],
                                    rqb[:, tc * 512:(tc + 1) * 512],
                                    start=True, stop=True, dep=[r4, lrp] + out_toks[-1:])
                    q0 = P.v.tensor_tensor(qnb[:, :], src[:, c * LHF + tc * 512: c * LHF + tc * 512 + 512],
                                           pm[0][:, :], ALU.mult, dep=[b1] + out_toks[-1:])
                    s2 = P.s.activation(sqb[:, :], qnb[:, :], AF.Square, scale=scol2[:, 0:1],
                                        dep=[q0] + lsc2 + [nadd])
                    nm = P.t.matmul(p16n[:, :], hm8[:, c * 16:(c + 1) * 16], sqb[:, :],
                                    start=True, stop=True, dep=[s2, lhm, nadd])
                    nadd = P.v.tensor_add(nsum[:, tc * 512:(tc + 1) * 512],
                                          nsum[:, tc * 512:(tc + 1) * 512], p16n[:, :],
                                          dep=[nm, nz])
                    r0 = P.t.matmul(prot[:, :], rotm[:, :], qnb[:, :], start=True, stop=True,
                                    dep=[q0, lrot])
                    t1 = P.v.tensor_tensor(t1b[:, :], qnb[:, :],
                                           cs4[:, tabq * LHF + tc * 512: tabq * LHF + tc * 512 + 512],
                                           ALU.mult, dep=[r0] + tt_toks)
                    t2 = P.v.tensor_tensor(t2b[:, :], prot[:, :],
                                           cs4[:, (tabq + 1) * LHF + tc * 512: (tabq + 1) * LHF + tc * 512 + 512],
                                           ALU.mult, dep=[t1] + tt_toks)
                    out_toks.append(P.v.tensor_add(
                        src[:, c * LHF + tc * 512: c * LHF + tc * 512 + 512], t1b[:, :], t2b[:, :],
                        dep=t2))
            r5 = P.s.activation(nsum[:, :], nsum[:, :], AF.Sqrt, dep=[nadd])
            r6 = P.v.tensor_scalar(nsum[:, :], nsum[:, :], qfac, None, op0=ALU.mult, dep=r5)
            r7 = P.v.tensor_reduce(wmx[:, 0:10], nsum.rearrange("p (w s) -> p w s", s=256),
                                   axis=AX.X, op=ALU.max, dep=r6)
            stw = P.sy.dma_start(out=wm_out[:, :], in_=wmx[:, 0:10], dep=r7)
            out_toks.append(stw)
            return out_toks
        qr = normrot(qT, 0, [qtk[-1], ktk[-1]], qsc2, qwm_o, 8.0)
        kr = normrot(kT, 2, [qr[-1]], ksc2, kwm_o, 1.0)

        stq = P.sy.dma_start(out=qT_o.rearrange("(c p) t -> p c t", p=128),
                             in_=qT.rearrange("p (c l) -> p c l", c=8), dep=qr)
        stk = P.sy.dma_start(out=kT_o.rearrange("(c p) t -> p c t", p=128),
                             in_=kT.rearrange("p (c l) -> p c l", c=8), dep=kr)
        P.sy.nop(dep=[stq, stk] + vst[-1:])

    return pr.build(body)


def build_l2fb():
    pr = Prog("l2fb")
    q_in = pr.din("qfT", [D, LHF], BF16)
    k_in = pr.din("kfT", [D, LHF], BF16)
    v_in = pr.din("vtok", [LHF, D], BF16)
    wo_in = pr.din("wo", [D, D], F32)
    msk_in = pr.din("masks", [128, 4 * 256], F32)
    dlt_in = pr.din("maskd", [128, 4 * 256], F32)
    isf_in = pr.din("isf", [1, 16], F32)
    qwm_in = pr.din("qwm", [16, 10], F32)
    kwm_in = pr.din("kwm", [16, 10], F32)
    yr_in = pr.din("yr", [D, 2049], F32)
    ym_o = pr.dout("ymid", [D, 2049], F32)
    pr.constv(1e-30)

    qT = pr.sb("qT", [128, 8 * LHF], BF16)
    kT = pr.sb("kT", [128, 8 * LHF], BF16)
    vt = pr.sb("vt", [128, 20 * D], BF16)
    wo = pr.sb("wo", [128, 8 * D], BF16)
    msk = pr.sb("msk", [128, 4 * 256], F32)
    dlt = pr.sb("dlt", [128, 4 * 256], F32)
    isfb = pr.sb("isfb", [128, 16], F32)
    onescol = pr.sb("onescol", [128, 1], BF16)
    onesr = pr.sb("onesr", [1, 64], BF16)
    qwm = pr.sb("qwm", [16, 16], F32)
    kwm = pr.sb("kwm", [16, 16], F32)
    kw2 = pr.sb("kw2", [16, 16], F32)
    cmat = pr.sb("cmat", [16, 16], F32)
    ncb = pr.sb("ncb", [128, 160], F32)
    tmf = pr.sb("tmf", [128, 256], F32)
    ebuf = pr.sb("ebuf", [128, 4 * 256], BF16)
    den = pr.sb("den", [1, 256], F32)
    denb = pr.sb("denb", [1, 256], BF16)
    avs = pr.sb("avs", [64, 256], F32)
    avw = pr.sb("avw", [128, 8 * 256], BF16)
    att = pr.sb("att", [128, 256], F32)
    yrs = pr.sb("yrs", [128, 256], F32)
    cd_h = pr.nc.dram_tensor("cdh", [16, 10], F32, kind="Internal").ap()

    psA = pr.ps("psA", [128, 256], F32)
    pav = pr.ps("pav", [64, 256], F32)
    pden = pr.ps("pden", [1, 256], F32)
    pbc = pr.ps("pbc", [64, 256], F32)
    pwo = [pr.ps(f"pwo{i}", [128, 256], F32) for i in range(2)]

    def body(P):
        lq = P.g.dma_start(out=qT.rearrange("p (c l) -> p c l", c=8),
                           in_=q_in.rearrange("(c p) l -> p c l", p=128))
        lk = P.g.dma_start(out=kT.rearrange("p (c l) -> p c l", c=8),
                           in_=k_in.rearrange("(c p) l -> p c l", p=128))
        lv = P.g.dma_start(out=vt.rearrange("p (c l) -> p c l", c=20),
                           in_=v_in.rearrange("(c p) l -> p c l", p=128))
        lwo = P.g.dma_start(out=wo.rearrange("p (c o) -> p c o", c=8),
                            in_=wo_in.rearrange("(c p) o -> p c o", p=128))
        lm = P.sy.dma_start(out=msk[:, :], in_=msk_in[:, :])
        ld = P.sy.dma_start(out=dlt[:, :], in_=dlt_in[:, :])
        lisf = P.g.dma_start(out=isfb[:, :], in_=isf_in[0:1, :].to_broadcast([128, 16]))
        lqw = P.sy.dma_start(out=qwm[:, 0:10], in_=qwm_in[:, :])
        lkw = P.sy.dma_start(out=kwm[:, 0:10], in_=kwm_in[:, :])
        o1 = P.v.memset(onescol[:, :], 1.0)
        o2 = P.v.memset(onesr[:, :], 1.0, dep=o1)

        cz = P.v.memset(cmat[:, :], 0.0)
        k1 = P.v.tensor_max(kw2[:, 1:10], kwm[:, 0:9], kwm[:, 1:10], dep=[lkw, cz])
        c1 = P.v.tensor_mul(cmat[:, 1:10], qwm[:, 1:10], kw2[:, 1:10], dep=[k1, lqw, cz])
        c2 = P.v.tensor_scalar(cmat[:, :], cmat[:, :], -1.0, None, op0=ALU.mult, dep=c1)
        sd = P.sy.dma_start(out=cd_h[:, :], in_=cmat[:, 0:10], dep=c2)
        lcb = P.g.dma_start(out=ncb[:, 0:160],
                            in_=cd_h.rearrange("h w -> (h w)")[None, :].to_broadcast([128, 160]),
                            dep=sd)

        st_toks = []
        for w in range(1, 10):
            avdone = []
            for h in range(16):
                hc, hr = h // 2, 64 * (h % 2)
                edone = []
                for jt in range(4):
                    m1 = P.t.matmul(psA[:, :],
                                    kT[hr:hr + 64, hc * LHF + (w - 1) * 256 + jt * 128: hc * LHF + (w - 1) * 256 + (jt + 1) * 128],
                                    qT[hr:hr + 64, hc * LHF + w * 256: hc * LHF + (w + 1) * 256],
                                    start=True, stop=True,
                                    dep=[lq, lk] + (edone[-1:] if edone else avdone[-1:] if avdone else st_toks[-1:]))
                    m2 = P.v.tensor_add(tmf[:, :], psA[:, :], msk[:, jt * 256:(jt + 1) * 256], dep=[m1, lm])
                    m3 = P.v.scalar_tensor_tensor(tmf[:, :], dlt[:, jt * 256:(jt + 1) * 256],
                                                  isfb[:, w:w + 1], tmf[:, :],
                                                  op0=ALU.mult, op1=ALU.add, dep=[m2, ld, lisf])
                    edone.append(P.s.activation(ebuf[:, jt * 256:(jt + 1) * 256], tmf[:, :], AF.Exp,
                                                bias=ncb[:, h * 10 + w: h * 10 + w + 1], dep=[m3, lcb]))
                mt = None
                for jt in range(4):
                    vtile = vt[:, ((w - 1) * 2 + jt) * D + hc * 128 + hr: ((w - 1) * 2 + jt) * D + hc * 128 + hr + 64]
                    mt = P.t.matmul(pav[:, :], vtile, ebuf[:, jt * 256:(jt + 1) * 256],
                                    start=(jt == 0), stop=(jt == 3), dep=[edone[jt], lv] + ([mt] if mt else []))
                    mt = P.t.matmul(pden[:, :], onescol[:, 0:1], ebuf[:, jt * 256:(jt + 1) * 256],
                                    start=(jt == 0), stop=(jt == 3), dep=[mt])
                cd = P.v.tensor_scalar(den[:, :], pden[:, :], 1e-30, None, op0=ALU.add, dep=mt)
                cr = P.v.reciprocal(den[:, :], den[:, :], dep=cd)
                cb = P.v.tensor_copy(denb[:, :], den[:, :], dep=cr)
                ca = P.v.tensor_copy(avs[:, :], pav[:, :], dep=[mt, cb])
                bm = P.t.matmul(pbc[:, :], onesr[0:1, :], denb[0:1, :], start=True, stop=True,
                                dep=[cb, o2] + avdone[-1:])
                avdone.append(P.v.tensor_tensor(avw[hr:hr + 64, hc * 256:(hc + 1) * 256],
                                                avs[:, :], pbc[:, :], ALU.mult, dep=[bm, ca]))
            ystart = w * 256 - 511
            for mc in range(8):
                pp = pwo[mc % 2]
                mt = None
                for kc in range(8):
                    mt = P.t.matmul(pp[:, :], wo[:, kc * D + mc * 128: kc * D + mc * 128 + 128],
                                    avw[:, kc * 256:(kc + 1) * 256],
                                    start=(kc == 0), stop=(kc == 7),
                                    dep=[lwo] + avdone + (st_toks[-2:-1] if len(st_toks) >= 2 else []))
                if w == 1:
                    lyr = P.sy.dma_start(out=yrs[:, 0:1], in_=yr_in[mc * 128:(mc + 1) * 128, 0:1],
                                         allow_slow_non_contiguous=True,
                                         dep=st_toks[-2:-1] if len(st_toks) >= 2 else None)
                    a1 = P.v.tensor_tensor(att[:, 0:1], pp[:, 255:256], yrs[:, 0:1], ALU.add,
                                           dep=[mt, lyr] + st_toks[-1:])
                    st_toks.append(P.sy.dma_start(out=ym_o[mc * 128:(mc + 1) * 128, 0:1],
                                                  in_=att[:, 0:1], allow_slow_non_contiguous=True, dep=a1))
                else:
                    lyr = P.sy.dma_start(out=yrs[:, :], in_=yr_in[mc * 128:(mc + 1) * 128, ystart:ystart + 256],
                                         dep=st_toks[-2:-1] if len(st_toks) >= 2 else None)
                    a1 = P.v.tensor_tensor(att[:, :], pp[:, :], yrs[:, :], ALU.add,
                                           dep=[mt, lyr] + st_toks[-1:])
                    st_toks.append(P.sy.dma_start(out=ym_o[mc * 128:(mc + 1) * 128, ystart:ystart + 256],
                                                  in_=att[:, :], dep=a1))
        P.sy.nop(dep=st_toks[-1])

    return pr.build(body)


def build_l2ff():
    pr = Prog("l2ff")
    DI2 = 2816
    ym_in = pr.din("ymid", [D, 2049], F32)
    w1_in = pr.din("w1p", [D, 2 * DI2], F32)
    b1_in = pr.din("b1p", [128, 44], F32)
    w2_in = pr.din("w2p", [DI2, D], F32)
    b2_in = pr.din("b2c", [128, 8], F32)
    g_in = pr.din("lng", [128, 8], F32)
    bb_in = pr.din("lnb", [128, 8], F32)
    yn_o = pr.dout("ynT", [D, TPC], F32)
    pr.constv(1e-5)

    xnT = pr.sb("xnT", [128, 8 * TPC], BF16)
    hT = pr.sb("hT", [128, 44 * 1024], BF16)
    ymb = hT[:, 0:8 * 2049]
    wbuf = [pr.sb(f"wbuf{i}", [128, 8 * 128], BF16) for i in range(2)]
    w2b = [pr.sb(f"w2b{i}", [128, 22 * 128], BF16) for i in range(2)]
    b1c = pr.sb("b1c", [128, 44], F32)
    b2c = pr.sb("b2c", [128, 8], F32)
    g8 = pr.sb("g8", [128, 8], F32)
    bb8 = pr.sb("bb8", [128, 8], F32)
    onesc = pr.sb("onesc", [128, 1], BF16)
    onesr = pr.sb("onesr", [1, 128], BF16)
    murow = pr.sb("murow", [1, TPC], F32)
    sqrow = pr.sb("sqrow", [1, TPC], F32)
    rstd = pr.sb("rstd", [1, TPC], F32)
    brow = pr.sb("brow", [1, TPC], F32)
    rstdb = pr.sb("rstdb", [1, TPC], BF16)
    browb = pr.sb("browb", [1, TPC], BF16)
    sqc = pr.sb("sqc", [128, 512], BF16)
    tchunk = pr.sb("tchunk", [128, 512], F32)
    yrs = pr.sb("yrs", [128, 512], F32)
    ost = pr.sb("ost", [128, 512], F32)

    p1r = pr.ps("p1r", [1, TPC], F32)
    pbc = pr.ps("pbc", [128, 512], F32)
    pm = [pr.ps(f"pm{i}", [128, 512], F32) for i in range(2)]

    def body(P):
        lym = P.g.dma_start(out=ymb.rearrange("p (c l) -> p c l", c=8),
                            in_=ym_in.rearrange("(c p) l -> p c l", p=128))
        lb1 = P.sy.dma_start(out=b1c[:, :], in_=b1_in[:, :])
        lg = P.sy.dma_start(out=g8[:, :], in_=g_in[:, :])
        lbb = P.sy.dma_start(out=bb8[:, :], in_=bb_in[:, :])
        lb2 = P.sy.dma_start(out=b2c[:, :], in_=b2_in[:, :])
        o1 = P.v.memset(onesc[:, :], 1.0)
        o2 = P.v.memset(onesr[:, :], 1.0, dep=o1)

        def ysl(c, j0, n):
            off = 1 if c < 4 else 0
            return ymb[:, c * 2049 + j0 + off: c * 2049 + j0 + off + n]

        mu_t = []
        for tc in range(4):
            mt = None
            for c in range(8):
                mt = P.t.matmul(p1r[0:1, tc * 512:(tc + 1) * 512], onesc[:, 0:1],
                                ysl(c, tc * 512, 512), start=(c == 0), stop=(c == 7),
                                dep=[lym, o1] + mu_t[-1:])
            mu_t.append(P.v.tensor_copy(murow[:, tc * 512:(tc + 1) * 512],
                                        p1r[0:1, tc * 512:(tc + 1) * 512], dep=mt))
        sq_t = list(mu_t)
        for tc in range(4):
            mt = None
            for c in range(8):
                s1 = P.s.activation(sqc[:, :], ysl(c, tc * 512, 512), AF.Square, dep=[lym] + sq_t[-1:])
                mt = P.t.matmul(p1r[0:1, tc * 512:(tc + 1) * 512], onesc[:, 0:1], sqc[:, :],
                                start=(c == 0), stop=(c == 7), dep=[s1])
            sq_t.append(P.v.tensor_copy(sqrow[:, tc * 512:(tc + 1) * 512],
                                        p1r[0:1, tc * 512:(tc + 1) * 512], dep=mt))
        v1 = P.v.tensor_scalar(murow[:, :], murow[:, :], 1.0 / D, None, op0=ALU.mult, dep=sq_t)
        v2 = P.v.tensor_scalar(sqrow[:, :], sqrow[:, :], 1.0 / D, None, op0=ALU.mult, dep=v1)
        v3 = P.v.tensor_mul(rstd[:, :], murow[:, :], murow[:, :], dep=v2)
        v4 = P.v.tensor_sub(sqrow[:, :], sqrow[:, :], rstd[:, :], dep=v3)
        v5 = P.s.activation(sqrow[:, :], sqrow[:, :], AF.Ln, bias=1e-5, dep=v4)
        v6 = P.s.activation(rstd[:, :], sqrow[:, :], AF.Exp, scale=-0.5, dep=v5)
        v7 = P.v.tensor_mul(brow[:, :], murow[:, :], rstd[:, :], dep=v6)
        v8 = P.v.tensor_scalar(brow[:, :], brow[:, :], -1.0, None, op0=ALU.mult, dep=v7)
        v9 = P.v.tensor_copy(rstdb[:, :], rstd[:, :], dep=v8)
        v10 = P.v.tensor_copy(browb[:, :], brow[:, :], dep=v9)
        xt = [v10]
        for c in range(8):
            for tc in range(4):
                b1 = P.t.matmul(pbc[:, :], onesr[0:1, :], rstdb[0:1, tc * 512:(tc + 1) * 512],
                                start=True, stop=True, dep=[v10, o2] + xt[-1:])
                x1 = P.v.tensor_tensor(tchunk[:, :], ysl(c, tc * 512, 512), pbc[:, :], ALU.mult,
                                       dep=[b1])
                b2_ = P.t.matmul(pbc[:, :], onesr[0:1, :], browb[0:1, tc * 512:(tc + 1) * 512],
                                 start=True, stop=True, dep=[x1])
                x2 = P.v.tensor_tensor(tchunk[:, :], tchunk[:, :], pbc[:, :], ALU.add, dep=[b2_])
                x3 = P.v.tensor_scalar(tchunk[:, :], tchunk[:, :], g8[:, c:c + 1], bb8[:, c:c + 1],
                                       op0=ALU.mult, op1=ALU.add, dep=[x2, lg, lbb])
                xt.append(P.v.tensor_copy(xnT[:, c * TPC + tc * 512: c * TPC + tc * 512 + 512],
                                          tchunk[:, :], dep=x3))

        st_toks = []
        for th in range(2):
            tcol = slice(th * 1024, (th + 1) * 1024)
            h_t = []
            wld = [None, None]
            for mc in range(44):
                wb = wbuf[mc % 2]
                lw = P.g.dma_start(out=wb.rearrange("p (c o) -> p c o", c=8),
                                   in_=w1_in.rearrange("(c p) o -> p c o", p=128)[:, :, mc * 128:(mc + 1) * 128],
                                   dep=[wld[mc % 2]] if wld[mc % 2] else st_toks[-1:])
                for ci in range(2):
                    pp = pm[ci]
                    mt = None
                    for kc in range(8):
                        mt = P.t.matmul(pp[:, :], wb[:, kc * 128:(kc + 1) * 128],
                                        xnT[:, kc * TPC + th * 1024 + ci * 512: kc * TPC + th * 1024 + ci * 512 + 512],
                                        start=(kc == 0), stop=(kc == 7),
                                        dep=[lw] + xt[-1:] + (h_t[-2:-1] if len(h_t) >= 2 else []))
                    if mc < 22:
                        h_t.append(P.v.tensor_scalar(hT[:, mc * 1024 + ci * 512: mc * 1024 + ci * 512 + 512],
                                                     pp[:, :], b1c[:, mc:mc + 1], None,
                                                     op0=ALU.add, dep=[mt, lb1]))
                    else:
                        h_t.append(P.s.activation(hT[:, mc * 1024 + ci * 512: mc * 1024 + ci * 512 + 512],
                                                  pp[:, :], AF.Gelu, bias=b1c[:, mc:mc + 1], dep=[mt, lb1]))
                wld[mc % 2] = h_t[-1]
            pr_t = []
            for mc in range(22):
                pr_t.append(P.v.tensor_mul(hT[:, mc * 1024:(mc + 1) * 1024],
                                           hT[:, mc * 1024:(mc + 1) * 1024],
                                           hT[:, (mc + 22) * 1024:(mc + 23) * 1024],
                                           dep=h_t + pr_t[-1:]))
            w2ld = [None, None]
            for mc in range(8):
                wb2 = w2b[mc % 2]
                lw2 = P.g.dma_start(out=wb2.rearrange("p (c o) -> p c o", c=22),
                                    in_=w2_in.rearrange("(c p) o -> p c o", p=128)[:, :, mc * 128:(mc + 1) * 128],
                                    dep=[w2ld[mc % 2]] if w2ld[mc % 2] else pr_t[-1:])
                for ci in range(2):
                    pp = pm[ci]
                    mt = None
                    for kc in range(22):
                        mt = P.t.matmul(pp[:, :], wb2[:, kc * 128:(kc + 1) * 128],
                                        hT[:, kc * 1024 + ci * 512: kc * 1024 + ci * 512 + 512],
                                        start=(kc == 0), stop=(kc == 21),
                                        dep=[lw2] + pr_t + (st_toks[-2:-1] if len(st_toks) >= 2 else []))
                    lyr = P.sy.dma_start(out=yrs[:, :],
                                         in_=ym_in[mc * 128:(mc + 1) * 128,
                                                   1 + th * 1024 + ci * 512: 1 + th * 1024 + ci * 512 + 512],
                                         dep=st_toks[-2:-1] if len(st_toks) >= 2 else None)
                    a1 = P.v.tensor_scalar(ost[:, :], pp[:, :], b2c[:, mc:mc + 1], None,
                                           op0=ALU.add, dep=[mt, lb2, lyr])
                    a1 = P.v.tensor_add(ost[:, :], ost[:, :], yrs[:, :], dep=a1)
                    w2ld[mc % 2] = a1
                    st_toks.append(P.sy.dma_start(
                        out=yn_o[mc * 128:(mc + 1) * 128, th * 1024 + ci * 512: th * 1024 + ci * 512 + 512],
                        in_=ost[:, :], dep=a1))
        P.sy.nop(dep=st_toks[-1])

    return pr.build(body)


ROTM = np.zeros((128, 128), dtype=NBF)
for _b in range(2):
    for _c in range(32):
        ROTM[_b * 64 + 32 + _c, _b * 64 + _c] = -1.0
        ROTM[_b * 64 + _c, _b * 64 + 32 + _c] = 1.0
HM8 = np.zeros((128, 8 * 16), dtype=NBF)
for _c in range(8):
    for _h in range(16):
        if _h // 2 == _c:
            HM8[64 * (_h % 2):64 * (_h % 2) + 64, _c * 16 + _h] = 1.0
REPL16 = np.zeros((16, 8 * 128), dtype=NBF)
for _c in range(8):
    for _h in (2 * _c, 2 * _c + 1):
        REPL16[_h, _c * 128 + 64 * (_h % 2): _c * 128 + 64 * (_h % 2) + 64] = 1.0
_M = np.zeros((512, 256), dtype=np.float32)
_DL = np.zeros((512, 256), dtype=np.float32)
for _j in range(512):
    for _i in range(256):
        if not (_i <= _j <= _i + 256):
            _M[_j, _i] = -3.4e38
        if _j < 256:
            _DL[_j, _i] = -3.4e38
MASKS4 = np.ascontiguousarray(_M.reshape(4, 128, 256).transpose(1, 0, 2).reshape(128, 4 * 256))
MASKD4 = np.ascontiguousarray(_DL.reshape(4, 128, 256).transpose(1, 0, 2).reshape(128, 4 * 256))


def _halo_featmaj(arr_T, c, halo, fill=0.0):
    Dd = arr_T.shape[0]
    out = np.full((Dd, halo + TPC), fill, arr_T.dtype)
    lo = c * TPC - halo
    if c % 2 == 0:
        out[:, halo:] = arr_T[:, c * TPC:(c + 1) * TPC]
    else:
        out[:, :] = arr_T[:, lo:(c + 1) * TPC]
    return np.ascontiguousarray(out)


def prep_l2fa(st, li):
    ins = st["in"]
    wq = f32c(ins["L2_ffn_Wqkv"][li])
    qs = f32c(ins["L2_ffn_q_scale"][li]).reshape(64, 1)
    ks = f32c(ins["L2_ffn_k_scale"][li]).reshape(64, 1)
    qs2 = np.concatenate([qs, np.concatenate([qs[32:], qs[:32]], 0)], 1)
    ks2 = np.concatenate([ks, np.concatenate([ks[32:], ks[:32]], 0)], 1)
    yT = st["l2_yT"]
    maps = []
    for c in range(NCORES):
        yh = _halo_featmaj(yT, c, 512)
        yfs = yh.copy()
        yfs[512:, 1:] = yh[512:, :-1]
        yfs[512:, 0] = 0.0
        if c % 2 == 1:
            yfs[512:, 0] = yT[512:, c * TPC - 513]
        maps.append({
            "yfs": f32c(yfs),
            "wqkv": wq,
            "tabs": f32c(st["l2_tabs"][c]),
            "qscol": f32c(qs2),
            "kscol": f32c(ks2),
            "rotm": ROTM,
            "hm8": HM8,
            "repl16": REPL16,
        })
    return maps


def post_l2fa(st, res):
    st["f_qfT"] = [r["qfT"] for r in res]
    st["f_kfT"] = [r["kfT"] for r in res]
    st["f_vtok"] = [r["vtok"] for r in res]
    st["f_qwm"] = [r["qwm"] for r in res]
    st["f_kwm"] = [r["kwm"] for r in res]


def prep_l2fb(st, li):
    ins = st["in"]
    yT = st["l2_yT"]
    maps = []
    for c in range(NCORES):
        isf = np.zeros((1, 16), np.float32)
        if c % 2 == 0:
            isf[0, 2] = 1.0
        yr = np.zeros((D, 2049), np.float32)
        t0 = c * TPC
        if c % 2 == 0:
            yr[:, 1:] = yT[:, t0:t0 + TPC]
        else:
            yr[:, :] = yT[:, t0 - 1:t0 + TPC]
        maps.append({
            "qfT": st["f_qfT"][c],
            "kfT": st["f_kfT"][c],
            "vtok": st["f_vtok"][c],
            "wo": f32c(ins["L2_ffn_Wo"][li]),
            "masks": MASKS4,
            "maskd": MASKD4,
            "isf": isf,
            "qwm": f32c(st["f_qwm"][c]),
            "kwm": f32c(st["f_kwm"][c]),
            "yr": yr,
        })
    return maps


def post_l2fb(st, res):
    st["f_ymid"] = [r["ymid"] for r in res]


def prep_l2ff(st, li):
    ins = st["in"]
    DI = 2730
    DI2 = 2816
    w1 = ins["L2_ffn_W1"][li]
    w1p = np.zeros((D, 2 * DI2), np.float32)
    w1p[:, 0:DI] = w1[:, 0:DI]
    w1p[:, DI2:DI2 + DI] = w1[:, DI:]
    b1 = ins["L2_ffn_b1"][li]
    b1p = np.zeros((2 * DI2,), np.float32)
    b1p[0:DI] = b1[0:DI]
    b1p[DI2:DI2 + DI] = b1[DI:]
    b1c = b1p.reshape(44, 128).T.copy()
    w2 = ins["L2_ffn_W2"][li]
    w2p = np.zeros((DI2, D), np.float32)
    w2p[0:DI] = w2
    b2c = ins["L2_ffn_b2"][li].reshape(8, 128).T.copy()
    lng = ins["L2_ffn_ln_g"][li].reshape(8, 128).T.copy()
    lnb = ins["L2_ffn_ln_b"][li].reshape(8, 128).T.copy()
    maps = []
    for c in range(NCORES):
        maps.append({
            "ymid": st["f_ymid"][c],
            "w1p": f32c(w1p),
            "b1p": f32c(b1c),
            "w2p": f32c(w2p),
            "b2c": f32c(b2c),
            "lng": f32c(lng),
            "lnb": f32c(lnb),
        })
    return maps


def post_l2ff(st, res):
    st["l2_yT"] = np.concatenate([r["ynT"] for r in res], 1)



def build_l3_pre():
    pr = Prog("l3_pre")
    NT = TPC // 128
    x2T_in = pr.din("x2T", [D, TPC], F32)
    a2T_in = pr.din("a2T", [D, TPC], BF16)
    yT_in = pr.din("yT", [D, TPC], F32)
    wq_in = pr.din("wq", [D, D], F32)
    wkv_in = pr.din("wkv", [D, 2 * D], F32)
    rw_in = pr.din("rw", [D, 8], F32)
    rb_in = pr.din("rb", [1, 1], F32)
    hm_in = pr.din("hm8", [128, 8 * 16], BF16)
    rp_in = pr.din("repl16", [16, 8 * 128], BF16)
    x3_o = pr.dout("x3T", [D, TPC], F32)
    xn_o = pr.dout("xn3T", [D, TPC], BF16)
    q_o = pr.dout("q3T", [D, TPC], BF16)
    k_o = pr.dout("k3T", [D, TPC], BF16)
    v_o = pr.dout("v3", [TPC, D], BF16)
    g_o = pr.dout("gates3", [TPC, 8], F32)
    pr.constv(1e-5)

    x3T = pr.sb("x3T", [128, 8 * TPC], F32)
    xnT = pr.sb("xnT", [128, 8 * TPC], BF16)
    x2b = [pr.sb(f"x2b{i}", [128, TPC], F32) for i in range(2)]
    a2b = [pr.sb(f"a2b{i}", [128, TPC], BF16) for i in range(2)]
    yb = [pr.sb(f"yb{i}", [128, TPC], F32) for i in range(2)]
    sqcf = pr.sb("sqcf", [128, 512], F32)
    murow = pr.sb("murow", [1, TPC], F32)
    sqrow = pr.sb("sqrow", [1, TPC], F32)
    trow = pr.sb("trow", [1, TPC], F32)
    onescF = pr.sb("onescF", [128, 1], F32)
    onesrF = pr.sb("onesrF", [1, 128], F32)
    tchf = pr.sb("tchf", [128, 512], F32)
    hm8 = pr.sb("hm8", [128, 8 * 16], BF16)
    repl = pr.sb("repl", [16, 8 * 128], BF16)
    wpb = [pr.sb(f"wpb{i}", [128, 8 * 128], BF16) for i in range(2)]
    wvb = pr.sb("wvb", [128, 8 * D], BF16)
    rwb = pr.sb("rwb", [128, 8 * 8], BF16)
    rbc = pr.sb("rbc", [128, 1], F32)
    qc = pr.sb("qc", [128, TPC], BF16)
    sqb = pr.sb("sqb", [128, 512], BF16)
    nrm = pr.sb("nrm", [16, 512], F32)
    nrmb = pr.sb("nrmb", [16, 512], BF16)
    qnc = pr.sb("qnc", [128, TPC], BF16)
    vc = pr.sb("vc", [128, 512], BF16)
    rl = pr.sb("rl", [128, 8], F32)
    mx8 = pr.sb("mx8", [128, 8], F32)
    rmax = pr.sb("rmax", [128, 1], F32)
    ebuf8 = pr.sb("ebuf8", [128, 8], F32)
    rsum = pr.sb("rsum", [128, 1], F32)
    corr = pr.sb("corr", [128, 8], F32)
    mskb = pr.sb("mskb", [128, 8], F32)
    gtile = pr.sb("gtile", [128, 8], F32)

    pm = [pr.ps(f"pm{i}", [128, 512], F32) for i in range(2)]
    p1r = pr.ps("p1r", [1, 512], F32)
    p16n = pr.ps("p16n", [16, 512], F32)
    pbcA = pr.ps("pbcA", [128, 512], F32)
    pbcB = pr.ps("pbcB", [128, 512], F32)

    def body(P):
        lhm = P.sy.dma_start(out=hm8[:, :], in_=hm_in[:, :])
        lrp = P.sy.dma_start(out=repl[:, :], in_=rp_in[:, :])
        lrw = P.g.dma_start(out=rwb.rearrange("p (c e) -> p c e", c=8),
                            in_=rw_in.rearrange("(c p) e -> p c e", p=128))
        lrb = P.g.dma_start(out=rbc[:, :], in_=rb_in[0:1, :].to_broadcast([128, 1]))
        lwv = P.g.dma_start(out=wvb.rearrange("p (c o) -> p c o", c=8),
                            in_=wkv_in.rearrange("(c p) o -> p c o", p=128)[:, :, D:2 * D])
        o1 = P.v.memset(onescF[:, :], 1.0)
        o2 = P.v.memset(onesrF[:, :], 1.0, dep=o1)

        xa = []
        ld = [None, None]
        st_x3 = []
        for c in range(8):
            dp = [ld[c % 2]] if ld[c % 2] else None
            l1 = P.sy.dma_start(out=x2b[c % 2][:, :], in_=x2T_in[c * 128:(c + 1) * 128, :], dep=dp)
            l2 = P.sy.dma_start(out=a2b[c % 2][:, :], in_=a2T_in[c * 128:(c + 1) * 128, :], dep=dp)
            l3 = P.sy.dma_start(out=yb[c % 2][:, :], in_=yT_in[c * 128:(c + 1) * 128, :], dep=dp)
            a1 = P.v.tensor_add(x3T[:, c * TPC:(c + 1) * TPC], x2b[c % 2][:, :], a2b[c % 2][:, :],
                                dep=[l1, l2])
            a2 = P.v.tensor_add(x3T[:, c * TPC:(c + 1) * TPC], x3T[:, c * TPC:(c + 1) * TPC],
                                yb[c % 2][:, :], dep=[a1, l3])
            ld[c % 2] = a2
            xa.append(a2)
            st_x3.append(P.sy.dma_start(out=x3_o[c * 128:(c + 1) * 128, :],
                                        in_=x3T[:, c * TPC:(c + 1) * TPC], dep=a2))

        mu_t = []
        for tc in range(4):
            mt = None
            for c in range(8):
                mt = P.t.matmul(p1r[0:1, :], onescF[:, 0:1], x3T[:, c * TPC + tc * 512: c * TPC + tc * 512 + 512],
                                start=(c == 0), stop=(c == 7), dep=[o1, xa[c]] + mu_t[-1:])
            mu_t.append(P.v.tensor_copy(murow[:, tc * 512:(tc + 1) * 512], p1r[0:1, :], dep=mt))
        sq_t = list(mu_t)
        for tc in range(4):
            mt = None
            for c in range(8):
                s1 = P.s.activation(sqcf[:, :], x3T[:, c * TPC + tc * 512: c * TPC + tc * 512 + 512],
                                    AF.Square, dep=[xa[c]] + sq_t[-1:] + ([mt] if mt else []))
                mt = P.t.matmul(p1r[0:1, :], onescF[:, 0:1], sqcf[:, :],
                                start=(c == 0), stop=(c == 7), dep=[s1])
            sq_t.append(P.v.tensor_copy(sqrow[:, tc * 512:(tc + 1) * 512], p1r[0:1, :], dep=mt))
        v1 = P.v.tensor_scalar(murow[:, :], murow[:, :], 1.0 / D, None, op0=ALU.mult, dep=sq_t)
        v2 = P.v.tensor_scalar(sqrow[:, :], sqrow[:, :], 1.0 / D, None, op0=ALU.mult, dep=v1)
        v3 = P.v.tensor_mul(trow[:, :], murow[:, :], murow[:, :], dep=v2)
        v4 = P.v.tensor_sub(sqrow[:, :], sqrow[:, :], trow[:, :], dep=v3)
        v5 = P.s.activation(sqrow[:, :], sqrow[:, :], AF.Ln, bias=1e-5, dep=v4)
        v6 = P.s.activation(sqrow[:, :], sqrow[:, :], AF.Exp, scale=-0.5, dep=v5)
        v7 = P.v.tensor_mul(trow[:, :], murow[:, :], sqrow[:, :], dep=v6)
        v8 = P.v.tensor_scalar(trow[:, :], trow[:, :], -1.0, None, op0=ALU.mult, dep=v7)

        xt = [v8]
        st_xn = []
        for c in range(8):
            for tc in range(4):
                b1 = P.t.matmul(pbcA[:, :], onesrF[0:1, :], sqrow[0:1, tc * 512:(tc + 1) * 512],
                                start=True, stop=True, dep=[v8, o2] + xt[-1:])
                x1 = P.v.tensor_tensor(tchf[:, :], x3T[:, c * TPC + tc * 512: c * TPC + tc * 512 + 512],
                                       pbcA[:, :], ALU.mult, dep=[b1])
                b2 = P.t.matmul(pbcB[:, :], onesrF[0:1, :], trow[0:1, tc * 512:(tc + 1) * 512],
                                start=True, stop=True, dep=[x1])
                x2_ = P.v.tensor_tensor(tchf[:, :], tchf[:, :], pbcB[:, :], ALU.add, dep=[b2])
                xt.append(P.v.tensor_copy(xnT[:, c * TPC + tc * 512: c * TPC + tc * 512 + 512],
                                          tchf[:, :], dep=x2_))
            st_xn.append(P.sy.dma_start(out=xn_o[c * 128:(c + 1) * 128, :],
                                        in_=xnT[:, c * TPC:(c + 1) * TPC], dep=xt[-1]))

        NTL = TPC // 128
        gtoks = []
        for i in range(NTL):
            mt = None
            for kc in range(8):
                mt = P.t.matmul(pm[0][:, 0:8], xnT[:, kc * TPC + i * 128: kc * TPC + (i + 1) * 128],
                                rwb[:, kc * 8:(kc + 1) * 8], start=(kc == 0), stop=(kc == 7),
                                dep=[lrw] + xt[-1:] + gtoks[-1:])
            r0 = P.v.tensor_copy(rl[:, :], pm[0][:, 0:8], dep=[mt] + gtoks[-1:])
            r1 = P.v.tensor_reduce(rmax[:, 0:1], rl[:, :], axis=AX.X, op=ALU.max, dep=r0)
            r2 = P.v.tensor_scalar(rmax[:, :], rmax[:, :], -1.0, None, op0=ALU.mult, dep=r1)
            r3 = P.s.activation(ebuf8[:, :], rl[:, :], AF.Exp, bias=rmax[:, 0:1], dep=r2)
            r4 = P.v.tensor_reduce(rsum[:, 0:1], ebuf8[:, :], axis=AX.X, op=ALU.add, dep=r3)
            r5 = P.v.reciprocal(rsum[:, :], rsum[:, :], dep=r4)
            r6 = P.v.tensor_scalar(rl[:, :], ebuf8[:, :], rsum[:, 0:1], None, op0=ALU.mult, dep=r5)
            r7 = P.v.tensor_scalar(corr[:, :], rl[:, :], rbc[:, 0:1], None, op0=ALU.add, dep=[r6, lrb])
            m8 = P.v.max(out=mx8[:, :], in_=corr[:, :], dep=r7)
            k1 = P.v.tensor_tensor(mskb[:, :], corr[:, :], mx8[:, 3:4].to_broadcast([128, 8]),
                                   ALU.is_ge, dep=m8)
            w1 = P.v.tensor_mul(gtile[:, :], rl[:, :], mskb[:, :], dep=k1)
            w2 = P.v.tensor_reduce(rsum[:, 0:1], gtile[:, :], axis=AX.X, op=ALU.add, dep=[w1, r6])
            w3 = P.v.tensor_scalar(rsum[:, :], rsum[:, :], 1e-20, None, op0=ALU.add, dep=w2)
            w4 = P.v.reciprocal(rsum[:, :], rsum[:, :], dep=w3)
            w5 = P.v.tensor_scalar(gtile[:, :], gtile[:, :], rsum[:, 0:1], None, op0=ALU.mult, dep=w4)
            gtoks.append(P.sy.dma_start(out=g_o[i * 128:(i + 1) * 128, :], in_=gtile[:, :], dep=w5))

        def qkproj(col0, dst_o, dep0):
            toks = list(dep0)
            wld2 = [None, None]
            for mc in range(8):
                wb = wpb[mc % 2]
                lw = P.g.dma_start(out=wb.rearrange("p (c o) -> p c o", c=8),
                                   in_=(wq_in if col0 < D else wkv_in).rearrange("(c p) o -> p c o", p=128)[:, :, (col0 % D) + mc * 128: (col0 % D) + (mc + 1) * 128],
                                   dep=[wld2[mc % 2]] if wld2[mc % 2] else dep0)
                tts = []
                for tc in range(4):
                    pp = pm[tc % 2]
                    mt = None
                    for kc in range(8):
                        mt = P.t.matmul(pp[:, :], wb[:, kc * 128:(kc + 1) * 128],
                                        xnT[:, kc * TPC + tc * 512: kc * TPC + tc * 512 + 512],
                                        start=(kc == 0), stop=(kc == 7),
                                        dep=[lw] + xt[-1:] + toks[-2:] + (tts[-2:-1] if len(tts) >= 2 else []))
                    tts.append(P.s.copy(qc[:, tc * 512:(tc + 1) * 512], pp[:, :], dep=[mt] + toks[-1:]))
                ntoks = []
                for tc in range(4):
                    s2 = P.s.activation(sqb[:, :], qc[:, tc * 512:(tc + 1) * 512], AF.Square,
                                        dep=tts + ntoks[-1:])
                    nm = P.t.matmul(p16n[:, :], hm8[:, mc * 16:(mc + 1) * 16], sqb[:, :],
                                    start=True, stop=True, dep=[s2, lhm] + ntoks[-1:])
                    c0 = P.v.tensor_copy(nrm[:, :], p16n[:, :], dep=nm)
                    c1 = P.s.activation(nrm[:, :], nrm[:, :], AF.Sqrt, dep=c0)
                    c2 = P.v.tensor_scalar(nrm[:, :], nrm[:, :], 1e-12, None, op0=ALU.max, dep=c1)
                    c3 = P.v.reciprocal(nrm[:, :], nrm[:, :], dep=c2)
                    c4 = P.v.tensor_copy(nrmb[:, :], nrm[:, :], dep=c3)
                    b1 = P.t.matmul(pbcA[:, :], repl[:, mc * 128:(mc + 1) * 128], nrmb[:, :],
                                    start=True, stop=True, dep=[c4, lrp])
                    ntoks.append(P.v.tensor_tensor(qnc[:, tc * 512:(tc + 1) * 512],
                                                   qc[:, tc * 512:(tc + 1) * 512], pbcA[:, :],
                                                   ALU.mult, dep=[b1] + tts))
                toks.append(P.sy.dma_start(out=dst_o[mc * 128:(mc + 1) * 128, :],
                                           in_=qnc[:, :], dep=ntoks))
                wld2[mc % 2] = toks[-1]
            return toks
        qtk = qkproj(0, q_o, [xt[-1], gtoks[-1]])
        ktk = qkproj(D, k_o, [qtk[-1]])

        vtoks = [ktk[-1]]
        for i in range(NTL):
            for nc2 in range(2):
                pp = pm[nc2]
                mt = None
                for kc in range(8):
                    mt = P.t.matmul(pp[:, :], xnT[:, kc * TPC + i * 128: kc * TPC + (i + 1) * 128],
                                    wvb[:, kc * D + nc2 * 512: kc * D + nc2 * 512 + 512],
                                    start=(kc == 0), stop=(kc == 7),
                                    dep=[lwv] + xt[-1:] + vtoks[-2:])
                ev = P.s.copy(vc[:, :], pp[:, :], dep=[mt] + vtoks[-1:])
                vtoks.append(P.sy.dma_start(out=v_o[i * 128:(i + 1) * 128, nc2 * 512:(nc2 + 1) * 512],
                                            in_=vc[:, :], dep=ev))
        P.sy.nop(dep=[st_x3[-1], st_xn[-1], gtoks[-1], vtoks[-1]])

    return pr.build(body)


def prep_l3_pre(st):
    ins = st["in"]
    x2T = np.ascontiguousarray(f32c(st["x2"]).T)
    rb = f32c(ins["L3_ffn_router_bias"]).reshape(1, 1)
    maps = []
    for c in range(NCORES):
        sl = slice(c * TPC, (c + 1) * TPC)
        maps.append({
            "x2T": np.ascontiguousarray(x2T[:, sl]),
            "a2T": np.ascontiguousarray(st["l2_attnT"][:, sl]),
            "yT": np.ascontiguousarray(f32c(st["l2_yT"][:, sl])),
            "wq": f32c(ins["L3_attn_w_q"]),
            "wkv": f32c(ins["L3_attn_w_kv"]),
            "rw": f32c(ins["L3_ffn_router_w"]),
            "rb": rb,
            "hm8": HM8,
            "repl16": REPL16,
        })
    return maps


def post_l3_pre(st, res):
    st["x3T"] = np.concatenate([r["x3T"] for r in res], 1)
    st["xn3Ts"] = np.stack([r["xn3T"] for r in res], 0)
    st["q3T"] = np.concatenate([r["q3T"] for r in res], 1)
    st["k3T"] = np.concatenate([r["k3T"] for r in res], 1)
    st["v3"] = np.concatenate([r["v3"] for r in res], 0)
    st["gates3"] = np.concatenate([r["gates3"] for r in res], 0)



def build_l3_attn():
    pr = Prog("l3_attn")
    SEQ = 4096
    NJ = SEQ // 128
    qp_in = pr.din("qp", [512, SEQ], BF16)
    kp_in = pr.din("kp", [512, SEQ], BF16)
    vp_in = pr.din("vp", [8 * SEQ, 64], BF16)
    sc_in = pr.din("scp", [1, 8], F32)
    av_o = pr.dout("av", [512, SEQ], BF16)

    qsb = [pr.sb(f"qsb{i}", [64, SEQ], BF16) for i in range(2)]
    ksb = [pr.sb(f"ksb{i}", [64, SEQ], BF16) for i in range(2)]
    v65 = [pr.sb(f"v65_{i}", [128, NJ * 65], BF16) for i in range(2)]
    scrow = pr.sb("scrow", [1, 8], F32)
    scs = pr.sb("scs", [128, 8], F32)
    negsc = pr.sb("negsc", [128, 8], F32)
    ebuf = [pr.sb(f"ebuf{i}", [128, 512], BF16) for i in range(2)]
    denr = pr.sb("denr", [1, 512], F32)
    denb = pr.sb("denb", [1, 512], BF16)
    avs = pr.sb("avs", [64, 512], F32)
    avob = pr.sb("avob", [64, 512], BF16)
    onesr = pr.sb("onesr", [1, 64], BF16)
    sch = pr.nc.dram_tensor("sch", [1, 8], F32, kind="Internal").ap()

    psA = pr.ps("psA", [128, 512], F32)
    pav = pr.ps("pav", [65, 512], F32)
    pbc = pr.ps("pbc", [64, 512], F32)

    def body(P):
        lsc = P.sy.dma_start(out=scrow[:, :], in_=sc_in[:, :])
        e1 = P.s.activation(scrow[:, :], scrow[:, :], AF.Exp, dep=lsc)
        sd = P.sy.dma_start(out=sch[:, :], in_=scrow[:, :], dep=e1)
        lbc = P.g.dma_start(out=scs[:, :], in_=sch[0:1, :].to_broadcast([128, 8]), dep=sd)
        ng = P.v.tensor_scalar(negsc[:, :], scs[:, :], -1.0, None, op0=ALU.mult, dep=lbc)
        o1 = P.v.memset(onesr[:, :], 1.0)
        ov = [P.v.memset(v65[i].rearrange("p (c w) -> p c w", w=65)[:, :, 64:65], 1.0)
              for i in range(2)]

        lq = [None, None]
        lk = [None, None]
        lv = [None, None]
        lastq = [None, None]
        last_ex = None
        avtail = []
        for pair in range(8):
            bi = pair % 2
            dpq = [lastq[bi]] if lastq[bi] else None
            lq[bi] = P.sy.dma_start(out=qsb[bi][:, :], in_=qp_in[pair * 64:(pair + 1) * 64, :], dep=dpq)
            lk[bi] = P.sy.dma_start(out=ksb[bi][:, :], in_=kp_in[pair * 64:(pair + 1) * 64, :], dep=dpq)
            lv[bi] = P.g.dma_start(
                out=v65[bi].rearrange("p (c w) -> p c w", w=65)[:, :, 0:64],
                in_=vp_in[pair * SEQ:(pair + 1) * SEQ, :].rearrange("(c p) w -> p c w", p=128),
                dep=([lastq[bi]] if lastq[bi] else ov))
            for qt in range(8):
                mt = None
                for jc in range(NJ):
                    m1 = P.t.matmul(psA[:, :], ksb[bi][:, jc * 128:(jc + 1) * 128],
                                    qsb[bi][:, qt * 512:(qt + 1) * 512], start=True, stop=True,
                                    dep=[lq[bi], lk[bi]] + ([last_ex] if last_ex else avtail[-1:]))
                    ex = P.s.activation(ebuf[jc % 2][:, :], psA[:, :], AF.Exp,
                                        scale=scs[:, pair:pair + 1], bias=negsc[:, pair:pair + 1],
                                        dep=[m1, ng] + ([mt] if mt else []))
                    last_ex = ex
                    mt = P.t.matmul(pav[:, :], v65[bi][:, jc * 65:(jc + 1) * 65], ebuf[jc % 2][:, :],
                                    start=(jc == 0), stop=(jc == NJ - 1),
                                    dep=[ex, lv[bi]] + avtail)
                    avtail = []
                cd = P.v.tensor_scalar(denr[:, :], pav[64:65, :], 1e-30, None, op0=ALU.add, dep=mt)
                cr = P.v.reciprocal(denr[:, :], denr[:, :], dep=cd)
                cb = P.v.tensor_copy(denb[:, :], denr[:, :], dep=cr)
                ca = P.v.tensor_copy(avs[:, :], pav[0:64, :], dep=[mt, cb])
                bm = P.t.matmul(pbc[:, :], onesr[0:1, :], denb[0:1, :], start=True, stop=True,
                                dep=[cb, o1])
                av1 = P.v.tensor_tensor(avob[:, :], avs[:, :], pbc[:, :], ALU.mult, dep=[bm, ca])
                stv = P.sy.dma_start(out=av_o[pair * 64:(pair + 1) * 64, qt * 512:(qt + 1) * 512],
                                     in_=avob[:, :], dep=av1)
                avtail = [ca, av1, stv]
            lastq[bi] = stv
        P.sy.nop(dep=stv)

    return pr.build(body)


def prep_l3_attn(st):
    ins = st["in"]
    scale = f32c(ins["L3_attn_scale"]).reshape(1, 16)
    maps = []
    for c in range(NCORES):
        b = c // 2
        hbase = 8 * (c % 2)
        qp = np.empty((512, 4096), NBF)
        kp = np.empty((512, 4096), NBF)
        vp = np.empty((8 * 4096, 64), NBF)
        for pi in range(8):
            h = hbase + pi
            qp[pi * 64:(pi + 1) * 64] = st["q3T"][h * 64:(h + 1) * 64, b * 4096:(b + 1) * 4096]
            kp[pi * 64:(pi + 1) * 64] = st["k3T"][h * 64:(h + 1) * 64, b * 4096:(b + 1) * 4096]
            vp[pi * 4096:(pi + 1) * 4096] = st["v3"][b * 4096:(b + 1) * 4096, h * 64:(h + 1) * 64]
        maps.append({
            "qp": qp, "kp": kp, "vp": vp,
            "scp": np.ascontiguousarray(scale[:, hbase:hbase + 8]),
        })
    return maps


def post_l3_attn(st, res):
    av = np.empty((D, N_TOK), NBF)
    for c in range(NCORES):
        b = c // 2
        hbase = 8 * (c % 2)
        for pi in range(8):
            h = hbase + pi
            av[h * 64:(h + 1) * 64, b * 4096:(b + 1) * 4096] = res[c]["av"][pi * 64:(pi + 1) * 64]
    st["av3"] = av



def build_l3_out():
    pr = Prog("l3_out")
    NT = TPC // 128
    av_in = pr.din("avT", [D, TPC], BF16)
    xn_in = pr.din("xnT", [D, TPC], BF16)
    x3_in = pr.din("x3", [TPC, D], F32)
    g_in = pr.din("gates", [TPC, 8], F32)
    wo_in = pr.din("wout", [D, D], F32)
    bo_in = pr.din("bout", [1, D], F32)
    gw_in = pr.din("gw", [12 * D, 512], F32)
    uw_in = pr.din("uw", [12 * D, 512], F32)
    dw_in = pr.din("dw", [12 * 512, D], F32)
    fin_o = pr.dout("fin", [TPC, D], F32)

    xnT = pr.sb("xnT", [128, 8 * TPC], BF16)
    avb = pr.sb("avb", [128, 8 * TPC], BF16)
    xacc = pr.sb("xacc", [128, NT * D], F32)
    gts = pr.sb("gts", [128, NT * 8], F32)
    boutb = pr.sb("boutb", [128, D], F32)
    woutb = pr.sb("woutb", [128, 8 * D], BF16)
    gwb = [pr.sb(f"gwb{i}", [128, 8 * 512], BF16) for i in range(2)]
    uwb = [pr.sb(f"uwb{i}", [128, 8 * 512], BF16) for i in range(2)]
    dwb = [pr.sb(f"dwb{i}", [128, 4 * D], BF16) for i in range(2)]
    x3b_ = pr.sb("x3b", [128, D], F32)
    x3b = [x3b_, x3b_]
    actT = pr.sb("actT", [128, 4 * 512], BF16)
    gtmp = pr.sb("gtmp", [128, 512], BF16)
    tmp5 = pr.sb("tmp5", [128, 512], F32)

    pm = [pr.ps(f"pm{i}", [128, 512], F32) for i in range(2)]
    pd = pr.ps("pd", [128, 512], F32)

    def body(P):
        lxn = P.g.dma_start(out=xnT.rearrange("p (c l) -> p c l", c=8),
                            in_=xn_in.rearrange("(c p) l -> p c l", p=128))
        lav = P.g.dma_start(out=avb.rearrange("p (c l) -> p c l", c=8),
                            in_=av_in.rearrange("(c p) l -> p c l", p=128))
        lg = P.g.dma_start(out=gts.rearrange("p (i e) -> p i e", e=8),
                           in_=g_in.rearrange("(i p) e -> p i e", p=128))
        lbo = P.g.dma_start(out=boutb[:, :], in_=bo_in[0:1, :].to_broadcast([128, D]))
        lwo = P.g.dma_start(out=woutb.rearrange("p (c o) -> p c o", c=8),
                            in_=wo_in.rearrange("(c p) o -> p c o", p=128))

        seeds = []
        lx3 = [None, None]
        for i in range(NT):
            dp = seeds[-1:] if seeds else None
            lx = P.sy.dma_start(out=x3b[i % 2][:, :], in_=x3_in[i * 128:(i + 1) * 128, :], dep=dp)
            for nc2 in range(2):
                pp = pm[nc2]
                mt = None
                for kc in range(8):
                    mt = P.t.matmul(pp[:, :], avb[:, kc * TPC + i * 128: kc * TPC + (i + 1) * 128],
                                    woutb[:, kc * D + nc2 * 512: kc * D + nc2 * 512 + 512],
                                    start=(kc == 0), stop=(kc == 7),
                                    dep=[lav, lwo] + seeds[-2:])
                t1 = P.v.tensor_tensor(xacc[:, i * D + nc2 * 512: i * D + nc2 * 512 + 512],
                                       pp[:, :], boutb[:, nc2 * 512:(nc2 + 1) * 512], ALU.add,
                                       dep=[mt, lbo])
                seeds.append(P.v.tensor_add(xacc[:, i * D + nc2 * 512: i * D + nc2 * 512 + 512],
                                            xacc[:, i * D + nc2 * 512: i * D + nc2 * 512 + 512],
                                            x3b[i % 2][:, nc2 * 512:(nc2 + 1) * 512], dep=[t1, lx]))
            lx3[i % 2] = seeds[-1]

        wld = [None, None]
        prev = seeds[-1]
        for e in range(12):
            dp = [wld[e % 2]] if wld[e % 2] else seeds[-1:]
            lgw = P.g.dma_start(out=gwb[e % 2].rearrange("p (c o) -> p c o", c=8),
                                in_=gw_in.rearrange("(c p) o -> p c o", p=128)[:, 8 * e:8 * (e + 1), :],
                                dep=dp)
            luw = P.g.dma_start(out=uwb[e % 2].rearrange("p (c o) -> p c o", c=8),
                                in_=uw_in.rearrange("(c p) o -> p c o", p=128)[:, 8 * e:8 * (e + 1), :],
                                dep=dp)
            ldw = P.g.dma_start(out=dwb[e % 2].rearrange("p (c o) -> p c o", c=4),
                                in_=dw_in.rearrange("(c p) o -> p c o", p=128)[:, 4 * e:4 * (e + 1), :],
                                dep=dp)
            for i4 in range(4):
                acts = []
                for oc in range(4):
                    mt = None
                    for kc in range(8):
                        mt = P.t.matmul(pm[0][:, :], gwb[e % 2][:, kc * 512 + oc * 128: kc * 512 + (oc + 1) * 128],
                                        xnT[:, kc * TPC + i4 * 512: kc * TPC + (i4 + 1) * 512],
                                        start=(kc == 0), stop=(kc == 7),
                                        dep=[lgw, lxn] + acts[-1:] + ([prev] if oc == 0 else []))
                    sg = P.s.activation(gtmp[:, :], pm[0][:, :], AF.Silu, dep=[mt] + acts[-1:])
                    mt = None
                    for kc in range(8):
                        mt = P.t.matmul(pm[1][:, :], uwb[e % 2][:, kc * 512 + oc * 128: kc * 512 + (oc + 1) * 128],
                                        xnT[:, kc * TPC + i4 * 512: kc * TPC + (i4 + 1) * 512],
                                        start=(kc == 0), stop=(kc == 7),
                                        dep=[luw, lxn] + acts[-1:])
                    acts.append(P.v.tensor_tensor(actT[:, oc * 512:(oc + 1) * 512], gtmp[:, :],
                                                  pm[1][:, :], ALU.mult, dep=[sg, mt]))
                for ts in range(4):
                    it = i4 * 4 + ts
                    for nc2 in range(2):
                        mt = None
                        for oc in range(4):
                            mt = P.t.matmul(pd[:, :], actT[:, oc * 512 + ts * 128: oc * 512 + ts * 128 + 128],
                                            dwb[e % 2][:, oc * D + nc2 * 512: oc * D + nc2 * 512 + 512],
                                            start=(oc == 0), stop=(oc == 3),
                                            dep=[ldw] + acts + ([prev] if prev else []))
                        if e < 8:
                            t5 = P.v.tensor_scalar(tmp5[:, :], pd[:, :], gts[:, it * 8 + e: it * 8 + e + 1],
                                                   None, op0=ALU.mult, dep=[mt, lg, prev])
                            prev = P.v.tensor_add(xacc[:, it * D + nc2 * 512: it * D + nc2 * 512 + 512],
                                                  xacc[:, it * D + nc2 * 512: it * D + nc2 * 512 + 512],
                                                  tmp5[:, :], dep=[t5])
                        else:
                            prev = P.v.tensor_add(xacc[:, it * D + nc2 * 512: it * D + nc2 * 512 + 512],
                                                  xacc[:, it * D + nc2 * 512: it * D + nc2 * 512 + 512],
                                                  pd[:, :], dep=[mt, prev])
            wld[e % 2] = prev

        st_toks = []
        for i in range(NT):
            st_toks.append(P.sy.dma_start(out=fin_o[i * 128:(i + 1) * 128, :],
                                          in_=xacc[:, i * D:(i + 1) * D], dep=[prev]))
        P.sy.nop(dep=st_toks[-1])

    return pr.build(body)


def prep_l3_out(st):
    ins = st["in"]
    gw_all = np.concatenate([f32c(ins["L3_ffn_gate_w"]).reshape(8 * D, 512),
                             np.concatenate([f32c(ins["L3_ffn_s_gate_w"])[:, j * 512:(j + 1) * 512]
                                             for j in range(4)], 0)], 0)
    uw_all = np.concatenate([f32c(ins["L3_ffn_up_w"]).reshape(8 * D, 512),
                             np.concatenate([f32c(ins["L3_ffn_s_up_w"])[:, j * 512:(j + 1) * 512]
                                             for j in range(4)], 0)], 0)
    dw_all = np.concatenate([f32c(ins["L3_ffn_down_w"]).reshape(8 * 512, D),
                             f32c(ins["L3_ffn_s_down_w"])], 0)
    bout = f32c(ins["L3_attn_b_out"]).reshape(1, D)
    maps = []
    for c in range(NCORES):
        sl = slice(c * TPC, (c + 1) * TPC)
        maps.append({
            "avT": np.ascontiguousarray(st["av3"][:, sl]),
            "xnT": np.ascontiguousarray(st["xn3Ts"][c]),
            "x3": np.ascontiguousarray(st["x3T"][:, sl].T),
            "gates": np.ascontiguousarray(st["gates3"][sl]),
            "wout": f32c(ins["L3_attn_w_out"]),
            "bout": bout,
            "gw": gw_all, "uw": uw_all, "dw": dw_all,
        })
    return maps


def post_l3_out(st, res):
    st["final"] = np.concatenate([r["fin"] for r in res], 0).reshape(4, 4096, D)
